# version 02
# baseline (speedup 1.0000x reference)
import math
import contextlib
import numpy as np
import ml_dtypes
import concourse.bass as bass
import concourse.mybir as mybir
from concourse.bass_utils import run_bass_kernel_spmd

F32 = mybir.dt.float32
BF16 = mybir.dt.bfloat16
AXX = mybir.AxisListType.X
OP = mybir.AluOpType
ACT = mybir.ActivationFunctionType
AP = bass.AP

DIM = 2048
S = 2048
NT = 16
NU = 8
NH = 16
QR = 1536
KVR = 512
DN = 128
DR = 64
DV = 128
HF = 8192
EPS_RMS = 1e-6
EPS_LN1 = 1e-5
EPS_LN2 = 1e-6
QB = 128.0
CLIP_HI = 128.0 - 1e-6
NEG = -1.0e30
ISCALE = 1.0 / math.sqrt(DN + DR)

DEBUG_TAPS = ()
PHASES = 99


class Em:
    def __init__(self, nc):
        self.nc = nc
        self.ops = []
        self.sems = {}
        self.es = contextlib.ExitStack()

    def sem(self, name):
        h = self.es.enter_context(self.nc.semaphore(name))
        self.sems[name] = [h, 0]

    def val(self, name):
        return self.sems[name][1]

    def wait(self, eng, sem, val=None):
        v = self.sems[sem][1] if val is None else val
        h = self.sems[sem][0]
        if v > 0:
            self.ops.append((eng, lambda e: e.wait_ge(h, v)))

    def dma(self, q, out, in_, sem, **kw):
        h = self.sems[sem][0]
        self.sems[sem][1] += 16
        self.ops.append((q, lambda e: e.dma_start(out=out, in_=in_, **kw).then_inc(h, 16)))

    def dmat(self, q, out, in_, sem):
        h = self.sems[sem][0]
        self.sems[sem][1] += 16
        self.ops.append((q, lambda e: e.dma_start_transpose(out, in_).then_inc(h, 16)))

    def op(self, eng, fn, sem=None):
        if sem is None:
            self.ops.append((eng, fn))
        else:
            h = self.sems[sem][0]
            self.sems[sem][1] += 1
            self.ops.append((eng, fn))
            self.ops.append((eng, lambda e: e.drain().then_inc(h, 1)))

    def vpad(self, n=2):
        for _ in range(n):
            self.ops.append(("v", lambda e: e.tensor_copy(
                self.padbuf[:, 0:64], self.padbuf[:, 64:128])))

    def barrier(self):
        engs = ("sync", "gp", "sc", "v", "pe")
        sems = ("sq", "gq", "cq", "pe", "ve", "se")
        for e in engs:
            for s in sems:
                self.wait(e, s)

    def run(self, eng_obj, name):
        for e, fn in self.ops:
            if e == name:
                fn(eng_obj)

    def flush(self):
        if not self.ops:
            return
        with self.nc.Block() as block:
            @block.sync
            def _(sync):
                self.run(sync, "sync")

            @block.gpsimd
            def _(gp):
                self.run(gp, "gp")

            @block.scalar
            def _(sc):
                self.run(sc, "sc")

            @block.vector
            def _(v):
                self.run(v, "v")

            @block.tensor
            def _(pe):
                self.run(pe, "pe")
        self.ops = []


def build_nc(taps=(), phases=99):
    nc = bass.Bass("TRN2", target_bir_lowering=False, detect_race_conditions=False)
    d = {}

    def din(name, shape, dtype=F32):
        d[name] = nc.dram_tensor(name, list(shape), dtype, kind="ExternalInput")

    def dsc(name, shape, dtype=BF16):
        kind = "ExternalOutput" if name in taps else "Internal"
        d[name] = nc.dram_tensor(name, list(shape), dtype, kind=kind)

    din("x", [S, DIM]); din("xT", [DIM, 1024])
    din("wqd", [DIM, QR]); din("wqu3", [NH, QR, DN]); din("wqr", [QR, NH * DR])
    din("wkd", [DIM, KVR]); din("wkuk", [NH, KVR, DN]); din("wkuv4", [4, KVR, 4 * DV])
    din("wkr", [DIM, DR]); din("woT", [DIM, DIM])
    din("wfcT", [DIM, HF]); din("wpjT", [HF, DIM])
    din("qnw", [128, QR]); din("kvnw", [128, KVR])
    din("cosk", [S, 32]); din("sink", [S, 32])
    din("cosq", [1024, 32]); din("sinq", [1024, 32])
    din("ident", [128, 128]); din("onesm", [128, 128])
    din("maskq", [128, 256]); din("maske", [128, 128]); din("masko", [128, 128])
    din("cpool", [128, 8])
    d["out"] = nc.dram_tensor("out", [DIM, 1024], F32, kind="ExternalOutput")

    dsc("xn_sc", [S, DIM]); dsc("x2_sc", [1024, DIM]); dsc("xq_sc", [1024, DIM])
    dsc("qc_sc", [1024, QR]); dsc("kvc_sc", [S, KVR])
    dsc("qpe_sc", [1024, NH * 128]); dsc("kro_sc", [S, 128])
    dsc("mrow_sc", [8, 128]); dsc("o_sc", [1024, NH * DV])
    dsc("attn_sc", [DIM, 1024], F32)
    dsc("h_sc", [HF, 1024]); dsc("hq_sc", [1024, HF])
    dsc("t2pj_sc", [HF, DIM])
    dsc("dbg_sc", [128, 64], F32)

    em = Em(nc)
    with em.es:
        _build(nc, em, d, phases)
    return nc


def _build(nc, em, d, phases):
    es = em.es

    def sb(name, shape, dtype=BF16, stack=None):
        return (stack or es).enter_context(nc.sbuf_tensor(name, list(shape), dtype))

    for s in ("sq", "gq", "cq", "pe", "ve", "se"):
        em.sem(s)

    identb = sb("identb", [128, 128])
    identf = sb("identf", [128, 128], F32)
    cpool = sb("cpool_sb", [128, 8], F32)
    onesf = sb("onesf", [128, 128], F32)
    maskqb = sb("maskqb", [128, 256], F32)
    maskeb = sb("maskeb", [128, 128], F32)
    maskob = sb("maskob", [128, 128], F32)
    coskb = sb("coskb", [128, NT * 32], F32)
    sinkb = sb("sinkb", [128, NT * 32], F32)
    cosqb = sb("cosqb", [128, NU * 32], F32)
    sinqb = sb("sinqb", [128, NU * 32], F32)
    stats = sb("stats", [128, 64], F32)
    junk = sb("junk", [128, 2048])
    em.padbuf = sb("padbuf", [128, 128], F32)
    gacc = sb("gacc", [128, 144], F32)
    mhb = sb("mhb", [128, 8])
    mtr = sb("mtr", [8, 128])
    pb = [es.enter_context(nc.psum_tensor(f"pb{i}", [128, 512], F32)) for i in range(8)]

    em.op("v", lambda e: e.memset(em.padbuf[:, :], 0.0))
    em.op("v", lambda e: e.memset(stats[:, :], 0.0))
    em.op("v", lambda e: e.memset(gacc[:, :], 0.0))
    em.op("v", lambda e: e.memset(mhb[:, :], 0.0))
    em.op("v", lambda e: e.memset(mtr[:, :], 0.0))
    em.op("v", lambda e: e.memset(junk[:, :], 0.0), "ve")
    em.dma("gp", identb[:, :], d["ident"][:, :], "gq")
    em.dma("sync", identf[:, :], d["ident"][:, :], "sq")
    em.dma("sync", cpool[:, :], d["cpool"][:, :], "sq")
    em.dma("sync", onesf[:, :], d["onesm"][:, :], "sq")
    em.dma("sync", maskqb[:, :], d["maskq"][:, :], "sq")
    em.dma("sync", maskeb[:, :], d["maske"][:, :], "sq")
    em.dma("sync", maskob[:, :], d["masko"][:, :], "sq")
    for t in range(NT):
        em.dma("sync", coskb[:, t * 32:(t + 1) * 32], d["cosk"][t * 128:(t + 1) * 128, :], "sq")
        em.dma("sync", sinkb[:, t * 32:(t + 1) * 32], d["sink"][t * 128:(t + 1) * 128, :], "sq")
    for u in range(NU):
        em.dma("sync", cosqb[:, u * 32:(u + 1) * 32], d["cosq"][u * 128:(u + 1) * 128, :], "sq")
        em.dma("sync", sinqb[:, u * 32:(u + 1) * 32], d["sinq"][u * 128:(u + 1) * 128, :], "sq")

    sl = {}
    for i, n in enumerate(("sum", "ssq", "mean", "var", "rstd", "t1", "m2", "r2",
                           "mx", "mn", "g", "s1", "s2", "nm", "gam_f", "gam_p",
                           "th_f", "th_p", "nth_f", "nth_p")):
        sl[n] = stats[:, i:i + 1]
    tmp4 = stats[:, 20:24]
    lhat = stats[:, 24:32]
    linv = stats[:, 32:40]
    mhf = stats[:, 40:48]

    def ssq_op(e, out_ap, src, acc):
        return e.scalar_tensor_tensor(out_ap, src, 1.0, src, OP.mult, OP.mult,
                                      accum_out=acc)

    if phases >= 1:
        qnwb = sb("qnwb", [128, QR], F32)
        kvnwb = sb("kvnwb", [128, KVR], F32)
        em.dma("sync", qnwb[:, :], d["qnw"][:, :], "sq")
        em.dma("sync", kvnwb[:, :], d["kvnw"][:, :], "sq")
        p0 = contextlib.ExitStack()
        with p0:
            xt_b = [sb(f"xt{i}", [128, DIM], F32, p0) for i in range(2)]
            xnf_b = [sb(f"xnf{i}", [128, DIM], F32, p0) for i in range(2)]
            uq = sb("uq", [128, DIM], F32, p0)
            x2o_b = [sb(f"x2o{i}", [64, DIM], BF16, p0) for i in range(2)]
            g_done = [0, 0]
            s_done = [0, 0]
            R = slice(64, 128)
            for t in range(NT):
                bi = t % 2
                xt, xnf, x2o = xt_b[bi], xnf_b[bi], x2o_b[bi]
                em.wait("sync", "gq", g_done[bi])
                em.wait("sync", "ve")
                em.dma("sync", xt[:, :], d["x"][t * 128:(t + 1) * 128, :], "sq")
                em.wait("v", "sq")
                em.wait("v", "se")
                em.op("v", lambda e, xt=xt: e.tensor_scalar(
                    junk[:, :], xt[:, :], 1.0, None, OP.mult, OP.add, accum_out=sl["sum"]))
                em.op("v", lambda e, xt=xt: ssq_op(e, junk[:, :], xt[:, :], sl["ssq"]))
                em.vpad(1)
                em.op("v", lambda e: e.tensor_scalar_mul(sl["mean"], sl["sum"], 1.0 / DIM))
                em.op("v", lambda e: e.tensor_scalar_mul(sl["t1"], sl["ssq"], 1.0 / DIM))
                em.vpad(1)
                em.op("v", lambda e: e.tensor_scalar(sl["var"], sl["mean"], sl["mean"],
                                                     None, OP.mult, OP.bypass))
                em.vpad(2)
                em.op("v", lambda e: e.tensor_tensor(sl["var"], sl["t1"], sl["var"], OP.subtract), "ve")
                em.wait("sc", "ve")
                em.op("sc", lambda e: e.activation(sl["rstd"], sl["var"], ACT.Sqrt,
                                                   bias=cpool[:, 0:1], scale=1.0), "se")
                em.wait("v", "se")
                em.op("v", lambda e: e.reciprocal(sl["rstd"], sl["rstd"]))
                em.vpad(2)
                em.op("v", lambda e: e.tensor_scalar(sl["t1"], sl["rstd"], sl["mean"],
                                                     -1.0, OP.mult, OP.mult), "ve")
                em.wait("sc", "ve")
                em.wait("sc", "gq", g_done[bi])
                em.op("sc", lambda e, xt=xt, xnf=xnf: e.activation(
                    xnf[:, :], xt[:, :], ACT.Identity, bias=sl["t1"], scale=sl["rstd"]), "se")
                em.wait("gp", "se")
                em.dma("gp", d["xn_sc"][t * 128:(t + 1) * 128, :], xnf[:, :], "gq")
                em.dma("gp", d["xq_sc"][t * 64:(t + 1) * 64, :], xnf[64:128, :], "gq")
                g_done[bi] = em.val("gq")

                o = xnf[64:128, :]
                em.wait("v", "se")
                em.op("v", lambda e, o=o: e.tensor_scalar(
                    junk[64:128, :], o, 1.0, None, OP.mult, OP.add, accum_out=sl["sum"][R]))
                em.op("v", lambda e, o=o: ssq_op(e, junk[64:128, :], o, sl["ssq"][R]))
                em.op("v", lambda e, o=o: e.tensor_scalar(
                    junk[64:128, :], o, 1.0, None, OP.mult, OP.max, accum_out=sl["mx"][R]))
                em.op("v", lambda e, o=o: e.tensor_scalar(
                    junk[64:128, :], o, 1.0, None, OP.mult, OP.min, accum_out=sl["mn"][R]))
                em.op("v", lambda e: e.tensor_scalar_mul(sl["m2"][R], sl["sum"][R], 1.0 / DIM))
                em.op("v", lambda e: e.tensor_scalar_mul(sl["t1"][R], sl["ssq"][R], 1.0 / DIM))
                em.vpad(1)
                em.op("v", lambda e: e.tensor_scalar(sl["var"][R], sl["m2"][R], sl["m2"][R],
                                                     None, OP.mult, OP.bypass))
                em.vpad(2)
                em.op("v", lambda e: e.tensor_tensor(sl["var"][R], sl["t1"][R], sl["var"][R], OP.subtract), "ve")
                em.wait("sc", "ve")
                em.op("sc", lambda e: e.activation(sl["r2"][R], sl["var"][R], ACT.Sqrt,
                                                   bias=cpool[64:128, 1:2], scale=1.0), "se")
                em.wait("v", "se")
                em.op("v", lambda e: e.reciprocal(sl["r2"][R], sl["r2"][R]))
                em.op("v", lambda e: e.tensor_scalar(sl["g"][R], sl["mx"][R], sl["m2"][R],
                                                     None, OP.subtract, OP.bypass))
                em.op("v", lambda e: e.tensor_scalar(sl["t1"][R], sl["mn"][R], sl["m2"][R],
                                                     -1.0, OP.subtract, OP.mult))
                em.vpad(2)
                em.op("v", lambda e: e.tensor_tensor(sl["g"][R], sl["g"][R], sl["t1"][R], OP.max))
                em.vpad(2)
                em.op("v", lambda e: e.tensor_scalar(sl["g"][R], sl["g"][R], sl["r2"][R],
                                                     None, OP.mult, OP.bypass))
                em.vpad(2)
                em.op("v", lambda e: e.reciprocal(sl["s1"][R], sl["g"][R]))
                em.op("v", lambda e: e.tensor_scalar_mul(sl["s2"][R], sl["g"][R], 1.0 / QB))
                em.vpad(1)
                em.op("v", lambda e: e.tensor_scalar(sl["s1"][R], sl["s1"][R], sl["r2"][R],
                                                     QB, OP.mult, OP.mult))
                em.vpad(2)
                em.op("v", lambda e: e.tensor_scalar(sl["t1"][R], sl["s1"][R], sl["m2"][R],
                                                     -1.0, OP.mult, OP.mult), "ve")
                em.wait("sc", "ve")
                em.op("sc", lambda e, o=o: e.activation(uq[64:128, :], o, ACT.Identity,
                                                        bias=sl["t1"][R], scale=sl["s1"][R]), "se")
                em.wait("v", "se")
                em.wait("v", "sq", s_done[bi])
                em.op("v", lambda e: e.tensor_scalar(uq[64:128, :], uq[64:128, :],
                                                     CLIP_HI, -CLIP_HI, OP.min, OP.max))
                em.op("v", lambda e, x2o=x2o: e.tensor_scalar(
                    x2o[:, :], uq[64:128, :], sl["s2"][R], None, OP.mult, OP.bypass), "ve")
                em.wait("sync", "ve")
                em.dma("sync", d["x2_sc"][t * 64:(t + 1) * 64, :], x2o[:, :], "sq")
                s_done[bi] = em.val("sq")

            em.barrier()
            em.flush()
            if phases < 2:
                return

        p1 = contextlib.ExitStack()
        with p1:
            em.barrier()
            xn_fm = sb("xn_fm", [128, NT * DIM], BF16, p1)
            xq_fm = sb("xq_fm", [128, NT * 1024], BF16, p1)
            wqdb = sb("wqdb", [128, NT * QR], BF16, p1)
            wkdb = sb("wkdb", [128, NT * KVR], BF16, p1)
            wkrb = sb("wkrb", [128, NT * DR], BF16, p1)
            qco_b = [sb(f"qco{i}", [128, QR], BF16, p1) for i in range(2)]
            kvo_b = [sb(f"kvo{i}", [128, KVR], BF16, p1) for i in range(2)]
            kro_b = [sb(f"kro{i}", [128, 128], BF16, p1) for i in range(2)]

            em.wait("sync", "gq")
            for c in range(NT):
                em.dmat("sync", xn_fm[:, c * DIM:(c + 1) * DIM],
                        d["xn_sc"].ap()[:, c * 128:(c + 1) * 128], "sq")
                em.dmat("sync", xq_fm[:, c * 1024:(c + 1) * 1024],
                        d["xq_sc"].ap()[:, c * 128:(c + 1) * 128], "sq")
                em.dma("gp", wqdb[:, c * QR:(c + 1) * QR],
                       d["wqd"][c * 128:(c + 1) * 128, :], "gq")
                em.dma("gp", wkdb[:, c * KVR:(c + 1) * KVR],
                       d["wkd"][c * 128:(c + 1) * 128, :], "gq")
                em.dma("gp", wkrb[:, c * DR:(c + 1) * DR],
                       d["wkr"][c * 128:(c + 1) * 128, :], "gq")
            em.op("v", lambda e: e.memset(kro_b[0][:, 64:128], 0.0))
            em.op("v", lambda e: e.memset(kro_b[1][:, 64:128], 0.0), "ve")
            em.wait("pe", "sq")
            em.wait("pe", "gq")

            qc_done = [0, 0]
            for u in range(NU):
                bi = u % 2
                qco = qco_b[bi]
                for c in range(NT):
                    st, sp = (c == 0), (c == NT - 1)
                    lhs = AP(xq_fm, c * 1024 + u * 128, [[NT * 1024, 128], [1, 128]])
                    for j in range(3):
                        em.op("pe", lambda e, lhs=lhs, c=c, j=j, st=st, sp=sp: e.matmul(
                            pb[j][:, :], lhs,
                            wqdb[:, c * QR + j * 512: c * QR + (j + 1) * 512],
                            start=st, stop=sp))
                em.op("pe", lambda e: e.nop(), "pe")
                em.wait("sc", "pe")
                for j in range(3):
                    em.op("sc", lambda e, j=j: e.activation(
                        junk[:, j * 512:(j + 1) * 512], pb[j][:, :], ACT.Square,
                        accum_out=tmp4[:, j:j + 1]), "se")
                em.wait("v", "se")
                em.op("v", lambda e: e.tensor_reduce(sl["sum"], tmp4[:, 0:3], AXX, OP.add))
                em.vpad(2)
                em.op("v", lambda e: e.tensor_scalar_mul(sl["var"], sl["sum"], 1.0 / QR), "ve")
                em.wait("sc", "ve")
                em.op("sc", lambda e: e.activation(sl["rstd"], sl["var"], ACT.Sqrt,
                                                   bias=cpool[:, 2:3], scale=1.0), "se")
                em.wait("v", "se")
                em.op("v", lambda e: e.reciprocal(sl["rstd"], sl["rstd"]))
                em.vpad(2)
                em.op("v", lambda e: e.tensor_scalar_mul(sl["rstd"], sl["rstd"], ISCALE))
                em.vpad(2)
                em.wait("v", "sq", qc_done[bi])
                for j in range(3):
                    em.op("v", lambda e, j=j, qco=qco: e.scalar_tensor_tensor(
                        qco[:, j * 512:(j + 1) * 512], pb[j][:, :], sl["rstd"],
                        qnwb[:, j * 512:(j + 1) * 512], OP.mult, OP.mult),
                        "ve" if j == 2 else None)
                em.wait("sync", "ve")
                em.dma("sync", d["qc_sc"][u * 128:(u + 1) * 128, :], qco[:, :], "sq")
                qc_done[bi] = em.val("sq")
                em.wait("pe", "ve")

            kv_done = [0, 0]
            kr_done = [0, 0]
            for t in range(NT):
                bi = t % 2
                kvo, kro = kvo_b[bi], kro_b[bi]
                for c in range(NT):
                    st, sp = (c == 0), (c == NT - 1)
                    lhs = AP(xn_fm, c * DIM + t * 128, [[NT * DIM, 128], [1, 128]])
                    em.op("pe", lambda e, lhs=lhs, c=c, st=st, sp=sp: e.matmul(
                        pb[4][:, :], lhs, wkdb[:, c * KVR:(c + 1) * KVR],
                        start=st, stop=sp))
                    em.op("pe", lambda e, lhs=lhs, c=c, st=st, sp=sp: e.matmul(
                        pb[5][:, 0:DR], lhs, wkrb[:, c * DR:(c + 1) * DR],
                        start=st, stop=sp))
                em.op("pe", lambda e: e.nop(), "pe")
                em.wait("sc", "pe")
                em.op("sc", lambda e: e.activation(junk[:, 0:KVR], pb[4][:, :],
                                                   ACT.Square, accum_out=sl["sum"]), "se")
                em.wait("v", "se")
                em.op("v", lambda e: e.tensor_scalar_mul(sl["var"], sl["sum"], 1.0 / KVR), "ve")
                em.wait("sc", "ve")
                em.op("sc", lambda e: e.activation(sl["rstd"], sl["var"], ACT.Sqrt,
                                                   bias=cpool[:, 2:3], scale=1.0), "se")
                em.wait("v", "se")
                em.op("v", lambda e: e.reciprocal(sl["rstd"], sl["rstd"]))
                em.vpad(2)
                em.wait("v", "sq", kv_done[bi])
                em.op("v", lambda e, kvo=kvo: e.scalar_tensor_tensor(
                    kvo[:, :], pb[4][:, :], sl["rstd"], kvnwb[:, :], OP.mult, OP.mult))
                cosap = AP(coskb, t * 32, [[NT * 32, 128], [0, 1], [1, 32]])
                sinap = AP(sinkb, t * 32, [[NT * 32, 128], [0, 1], [1, 32]])

                def pap(off):
                    return AP(pb[5], off, [[512, 128], [64, 1], [2, 32]])

                def kp(tns, off):
                    return AP(tns, off, [[tns.shape[1], 128], [64, 1], [2, 32]])

                em.wait("v", "sq", kr_done[bi])
                em.op("v", lambda e, kro=kro, cosap=cosap: e.tensor_tensor(kp(kro, 0), pap(0), cosap, OP.mult))
                em.op("v", lambda e, kro=kro, cosap=cosap: e.tensor_tensor(kp(kro, 1), pap(1), cosap, OP.mult))
                em.op("v", lambda e, sinap=sinap: e.tensor_tensor(kp(junk, 0), pap(1), sinap, OP.mult))
                em.op("v", lambda e, sinap=sinap: e.tensor_tensor(kp(junk, 1), pap(0), sinap, OP.mult))
                em.op("v", lambda e, kro=kro: e.tensor_tensor(
                    kp(kro, 0), kp(kro, 0), kp(junk, 0), OP.subtract))
                em.op("v", lambda e, kro=kro: e.tensor_tensor(
                    kp(kro, 1), kp(kro, 1), kp(junk, 1), OP.add), "ve")
                em.wait("sync", "ve")
                em.dma("sync", d["kvc_sc"][t * 128:(t + 1) * 128, :], kvo[:, :], "sq")
                kv_done[bi] = em.val("sq")
                em.dma("sync", d["kro_sc"][t * 128:(t + 1) * 128, :], kro[:, :], "sq")
                kr_done[bi] = em.val("sq")
                em.wait("pe", "ve")

            em.flush()

        if phases < 3:
            return

        att = contextlib.ExitStack()
        with att:
            em.barrier()
            qc_fm = sb("qc_fm", [128, 12 * 1024], BF16, att)
            p15 = contextlib.ExitStack()
            with p15:
                wqrb = sb("wqrb", [128, 12 * 1024], BF16, p15)
                qpo_b = [sb(f"qpo{i}", [128, NH * 128], BF16, p15) for i in range(2)]
                for c in range(12):
                    em.dmat("sync", qc_fm[:, c * 1024:(c + 1) * 1024],
                            d["qc_sc"].ap()[:, c * 128:(c + 1) * 128], "sq")
                    em.dma("gp", wqrb[:, c * 1024:(c + 1) * 1024],
                           d["wqr"][c * 128:(c + 1) * 128, :], "gq")
                em.op("v", lambda e: e.memset(qpo_b[0][:, :], 0.0))
                em.op("v", lambda e: e.memset(qpo_b[1][:, :], 0.0), "ve")
                em.wait("pe", "sq")
                em.wait("pe", "gq")
                qp_done = [0, 0]
                for u in range(NU):
                    bi = u % 2
                    qpo = qpo_b[bi]
                    for c in range(12):
                        st, sp = (c == 0), (c == 11)
                        lhs = AP(qc_fm, c * 1024 + u * 128, [[12 * 1024, 128], [1, 128]])
                        em.op("pe", lambda e, lhs=lhs, st=st, sp=sp, c=c: e.matmul(
                            pb[6][:, :], lhs, wqrb[:, c * 1024: c * 1024 + 512],
                            start=st, stop=sp))
                        em.op("pe", lambda e, lhs=lhs, st=st, sp=sp, c=c: e.matmul(
                            pb[7][:, :], lhs, wqrb[:, c * 1024 + 512:(c + 1) * 1024],
                            start=st, stop=sp))
                    em.op("pe", lambda e: e.nop(), "pe")
                    em.wait("v", "pe")
                    em.wait("v", "sq", qp_done[bi])
                    cq_ = AP(cosqb, u * 32, [[NU * 32, 128], [0, 8], [1, 32]])
                    sq_ = AP(sinqb, u * 32, [[NU * 32, 128], [0, 8], [1, 32]])
                    for bk in range(2):
                        def pp(off, bk=bk):
                            return AP(pb[6 + bk], off, [[512, 128], [64, 8], [2, 32]])

                        def qp(off, bk=bk, qpo=qpo):
                            return AP(qpo, bk * 1024 + off, [[NH * 128, 128], [128, 8], [2, 32]])

                        def jp(off):
                            return AP(junk, off, [[2048, 128], [128, 8], [2, 32]])

                        em.op("v", lambda e, qp=qp, pp=pp, cq_=cq_: e.tensor_tensor(qp(0), pp(0), cq_, OP.mult))
                        em.op("v", lambda e, qp=qp, pp=pp, cq_=cq_: e.tensor_tensor(qp(1), pp(1), cq_, OP.mult))
                        em.op("v", lambda e, jp=jp, pp=pp, sq_=sq_: e.tensor_tensor(jp(0), pp(1), sq_, OP.mult))
                        em.op("v", lambda e, jp=jp, pp=pp, sq_=sq_: e.tensor_tensor(jp(1), pp(0), sq_, OP.mult))
                        em.op("v", lambda e, qp=qp, jp=jp: e.tensor_tensor(qp(0), qp(0), jp(0), OP.subtract))
                        em.op("v", lambda e, qp=qp, jp=jp: e.tensor_tensor(qp(1), qp(1), jp(1), OP.add), "ve")
                    em.wait("sync", "ve")
                    em.dma("sync", d["qpe_sc"][u * 128:(u + 1) * 128, :], qpo[:, :], "sq")
                    qp_done[bi] = em.val("sq")
                    em.wait("pe", "ve")

                em.flush()

            if phases < 4:
                return

            em.barrier()
            kvc_fm = sb("kvc_fm", [128, 4 * S], BF16, att)
            kroaug = sb("kroaug", [128, S], BF16, att)
            for c in range(4):
                em.dmat("sync", kvc_fm[:, c * S:(c + 1) * S],
                        d["kvc_sc"].ap()[:, c * 128:(c + 1) * 128], "sq")
            em.dmat("sync", kroaug[:, :], d["kro_sc"].ap()[:, :], "sq")
            em.wait("v", "sq")
            em.op("v", lambda e: e.memset(kroaug[64:65, :], -1.0), "ve")
            em.wait("pe", "sq")
            em.wait("pe", "ve")

            wqu_b = [sb(f"wqu{i}", [128, 12 * 128], BF16, att) for i in range(2)]
            wkuk_b = [sb(f"wkuk{i}", [128, 4 * 128], BF16, att) for i in range(2)]
            qpeT_b = [sb(f"qpeT{i}", [128, 1024], BF16, att) for i in range(2)]
            knop_b = [sb(f"knop{i}", [128, S], BF16, att) for i in range(2)]
            qnop_b = [sb(f"qnop{i}", [128, 1024], BF16, att) for i in range(2)]
            wkuv_sb = sb("wkuv_sb", [128, 4 * 512], BF16, att)
            vgrp = sb("vgrp", [128, NT * 512], BF16, att)
            pt = sb("pt", [128, NT * 1024], BF16, att)
            otile_b = [sb(f"otile{i}", [128, 128], BF16, att) for i in range(2)]
            ot_done = [0, 0]
            pv_done = [0, 0]

            def head_loads(hh):
                bi = hh % 2
                em.wait("gp", "pe")
                in_ap = AP(d["wqu3"], hh * QR * DN, [[DN, 128], [128 * DN, 12], [1, DN]])
                out_ap = AP(wqu_b[bi], 0, [[12 * 128, 128], [128, 12], [1, 128]])
                em.dma("gp", out_ap, in_ap, "gq")
                in_ap = AP(d["wkuk"], hh * KVR * DN, [[DN, 128], [128 * DN, 4], [1, DN]])
                out_ap = AP(wkuk_b[bi], 0, [[4 * 128, 128], [128, 4], [1, 128]])
                em.dma("gp", out_ap, in_ap, "gq")
                em.wait("sc", "pe")
                em.wait("sc", "sq")
                em.dmat("sc", qpeT_b[bi][:, :],
                        d["qpe_sc"].ap()[:, hh * 128:(hh + 1) * 128], "cq")

            def vgroup(g):
                in_ap = AP(d["wkuv4"], g * KVR * 512, [[512, 128], [128 * 512, 4], [1, 512]])
                out_ap = AP(wkuv_sb, 0, [[4 * 512, 128], [512, 4], [1, 512]])
                em.wait("gp", "pe")
                em.dma("gp", out_ap, in_ap, "gq")
                em.wait("pe", "gq")
                em.wait("pe", "se")
                vg_done = [0, 0]
                for tt in range(NT):
                    bank = 6 + tt % 2
                    if tt >= 2:
                        em.wait("pe", "se", vg_done[tt % 2])
                    for c in range(4):
                        lhs = AP(kvc_fm, c * S + tt * 128, [[4 * S, 128], [1, 128]])
                        em.op("pe", lambda e, lhs=lhs, c=c, bank=bank: e.matmul(
                            pb[bank][:, :], lhs, wkuv_sb[:, c * 512:(c + 1) * 512],
                            start=(c == 0), stop=(c == 3)))
                    em.op("pe", lambda e: e.nop(), "pe")
                    em.wait("sc", "pe")
                    em.op("sc", lambda e, tt=tt, bank=bank: e.copy(
                        vgrp[:, tt * 512:(tt + 1) * 512], pb[bank][:, :]), "se")
                    vg_done[tt % 2] = em.val("se")
                em.wait("pe", "se")

            head_loads(0)
            for hh in range(NH):
                bi = hh % 2
                wqu, wkuk_s = wqu_b[bi], wkuk_b[bi]
                qpeT, knop, qnop = qpeT_b[bi], knop_b[bi], qnop_b[bi]
                hin = hh % 4
                if hh % 4 == 0:
                    vgroup(hh // 4)
                if hh + 1 < NH:
                    head_loads(hh + 1)
                em.wait("pe", "gq")
                em.wait("pe", "cq")
                em.wait("pe", "se")
                for c in range(4):
                    for kb in range(4):
                        em.op("pe", lambda e, c=c, kb=kb, wkuk_s=wkuk_s: e.matmul(
                            pb[kb][:, :], wkuk_s[:, c * 128:(c + 1) * 128],
                            kvc_fm[:, c * S + kb * 512: c * S + (kb + 1) * 512],
                            start=(c == 0), stop=(c == 3)))
                em.op("pe", lambda e: e.nop(), "pe")
                em.wait("sc", "pe")
                for kb in range(4):
                    em.op("sc", lambda e, kb=kb, knop=knop: e.copy(
                        knop[:, kb * 512:(kb + 1) * 512], pb[kb][:, :]), "se")
                for c in range(12):
                    for jb in range(2):
                        em.op("pe", lambda e, c=c, jb=jb, wqu=wqu: e.matmul(
                            pb[4 + jb][:, :], wqu[:, c * 128:(c + 1) * 128],
                            qc_fm[:, c * 1024 + jb * 512: c * 1024 + (jb + 1) * 512],
                            start=(c == 0), stop=(c == 11)))
                em.op("pe", lambda e: e.nop(), "pe")
                em.wait("sc", "pe")
                for jb in range(2):
                    em.op("sc", lambda e, jb=jb, qnop=qnop: e.copy(
                        qnop[:, jb * 512:(jb + 1) * 512], pb[4 + jb][:, :]), "se")
                em.wait("pe", "se")

                for u in range(NU):
                    kmax = 256 * (u + 1)
                    nkb = (kmax + 511) // 512
                    for kb in range(nkb):
                        n = min(512, kmax - kb * 512)
                        em.op("pe", lambda e, u=u, kb=kb, n=n, qnop=qnop, knop=knop: e.matmul(
                            pb[kb][:, 0:n], qnop[:, u * 128:(u + 1) * 128],
                            knop[:, kb * 512: kb * 512 + n], start=True, stop=False))
                        em.op("pe", lambda e, u=u, kb=kb, n=n, qpeT=qpeT: e.matmul(
                            pb[kb][:, 0:n], qpeT[0:64, u * 128:(u + 1) * 128],
                            kroaug[0:64, kb * 512: kb * 512 + n], start=False, stop=True))
                    em.op("pe", lambda e: e.nop(), "pe")
                    em.wait("v", "pe")
                    em.wait("v", "se")
                    mb1 = (kmax - 256) // 512
                    off1 = (kmax - 256) % 512
                    if off1 + 256 <= 512:
                        em.op("v", lambda e, mb1=mb1, off1=off1: e.tensor_tensor(
                            pb[mb1][:, off1:off1 + 256], pb[mb1][:, off1:off1 + 256],
                            maskqb[:, :], OP.add))
                    else:
                        w1 = 512 - off1
                        em.op("v", lambda e, mb1=mb1, off1=off1, w1=w1: e.tensor_tensor(
                            pb[mb1][:, off1:512], pb[mb1][:, off1:512],
                            maskqb[:, 0:w1], OP.add))
                        em.op("v", lambda e, mb1=mb1, w1=w1: e.tensor_tensor(
                            pb[mb1 + 1][:, 0:256 - w1], pb[mb1 + 1][:, 0:256 - w1],
                            maskqb[:, w1:256], OP.add))
                    for kb in range(nkb):
                        n = min(512, kmax - kb * 512)
                        em.op("v", lambda e, kb=kb, n=n: e.tensor_reduce(
                            tmp4[:, kb:kb + 1], pb[kb][:, 0:n], AXX, OP.max))
                    em.vpad(3)
                    em.op("v", lambda e, nkb=nkb: e.tensor_reduce(
                        sl["nm"], tmp4[:, 0:nkb], AXX, OP.max, negate=True))
                    em.vpad(4)
                    em.op("v", lambda e, u=u: e.tensor_copy(mhb[:, u:u + 1], sl["nm"]))
                    em.vpad(4)
                    em.op("v", lambda e, u=u: e.tensor_copy(mhf[:, u:u + 1], mhb[:, u:u + 1]))
                    em.vpad(4)
                    em.op("v", lambda e, u=u: e.tensor_copy(sl["nm"], mhf[:, u:u + 1]), "ve")
                    em.wait("sc", "ve")
                    for kb in range(nkb):
                        n = min(512, kmax - kb * 512)
                        em.op("sc", lambda e, kb=kb, n=n: e.activation(
                            junk[:, kb * 512:kb * 512 + n], pb[kb][:, 0:n], ACT.Exp,
                            bias=sl["nm"], scale=1.0, accum_out=tmp4[:, kb:kb + 1]), "se")
                    em.wait("v", "se")
                    em.op("v", lambda e, u=u, nkb=nkb: e.tensor_reduce(
                        lhat[:, u:u + 1], tmp4[:, 0:nkb], AXX, OP.add), "ve")
                    em.wait("pe", "ve")
                em.vpad(4)
                em.op("v", lambda e: e.reciprocal(linv[:, :], lhat[:, :]))
                em.op("v", lambda e: e.tensor_scalar_mul(mhf[:, :], mhf[:, :], -1.0), "ve")
                em.wait("pe", "ve")
                em.op("pe", lambda e: e.transpose(pb[7][0:8, 0:128], mhf[:, :], identf[:, :]))
                em.op("pe", lambda e: e.nop(), "pe")
                em.wait("v", "pe")
                em.op("v", lambda e: e.tensor_copy(mtr[:, :], pb[7][0:8, 0:128]), "ve")
                em.wait("sc", "ve")
                em.dma("sc", d["mrow_sc"][:, :], mtr[:, :], "cq")
                em.wait("sc", "cq")
                em.dma("sc", qpeT[64:65, :], AP(d["mrow_sc"], 0, [[1024, 1], [1, 1024]]), "cq")
                em.wait("pe", "cq")

                for kt in range(NT):
                    u0 = kt // 2
                    q0 = u0 * 128
                    wq_ = 1024 - q0
                    nqb = (wq_ + 511) // 512
                    for qi in range(nqb):
                        n = min(512, wq_ - qi * 512)
                        bank = 4 + qi % 2
                        em.op("pe", lambda e, kt=kt, q0=q0, qi=qi, n=n, bank=bank,
                              knop=knop, qnop=qnop: e.matmul(
                                  pb[bank][:, 0:n], knop[:, kt * 128:(kt + 1) * 128],
                                  qnop[:, q0 + qi * 512: q0 + qi * 512 + n],
                                  start=True, stop=False))
                        em.op("pe", lambda e, kt=kt, q0=q0, qi=qi, n=n, bank=bank,
                              qpeT=qpeT: e.matmul(
                                  pb[bank][:, 0:n], kroaug[0:65, kt * 128:(kt + 1) * 128],
                                  qpeT[0:65, q0 + qi * 512: q0 + qi * 512 + n],
                                  start=False, stop=True))
                    em.op("pe", lambda e: e.nop(), "pe")
                    em.wait("v", "pe")
                    mk = maskeb if kt % 2 == 0 else maskob
                    em.op("v", lambda e, mk=mk: e.tensor_tensor(
                        pb[4][:, 0:128], pb[4][:, 0:128], mk[:, :], OP.add), "ve")
                    em.wait("sc", "ve")
                    for qi in range(nqb):
                        n = min(512, wq_ - qi * 512)
                        bank = 4 + qi % 2
                        em.op("sc", lambda e, kt=kt, q0=q0, qi=qi, n=n, bank=bank: e.activation(
                            pt[:, kt * 1024 + q0 + qi * 512: kt * 1024 + q0 + qi * 512 + n],
                            pb[bank][:, 0:n], ACT.Exp), "se")
                    em.wait("pe", "se")

                for u in range(NU):
                    bank = 6 + u % 2
                    nkt = 2 * u + 2
                    em.wait("pe", "se", pv_done[u % 2])
                    for kt in range(nkt):
                        em.op("pe", lambda e, u=u, kt=kt, bank=bank, hin=hin,
                              nkt=nkt: e.matmul(
                            pb[bank][:, 0:128],
                            pt[:, kt * 1024 + u * 128: kt * 1024 + u * 128 + 128],
                            vgrp[:, kt * 512 + hin * 128: kt * 512 + hin * 128 + 128],
                            start=(kt == 0), stop=(kt == nkt - 1)))
                    em.op("pe", lambda e: e.nop(), "pe")
                    em.wait("sc", "pe")
                    ot = otile_b[u % 2]
                    em.wait("sc", "cq", ot_done[u % 2])
                    em.op("sc", lambda e, u=u, bank=bank, ot=ot: e.activation(
                        ot[:, :], pb[bank][:, 0:128], ACT.Identity,
                        scale=linv[:, u:u + 1]), "se")
                    pv_done[u % 2] = em.val("se")
                    em.wait("sc", "se")
                    em.dma("sc", d["o_sc"][u * 128:(u + 1) * 128, hh * DV:(hh + 1) * DV],
                           ot[:, :], "cq")
                    ot_done[u % 2] = em.val("cq")
                em.wait("pe", "cq", ot_done[0])

            em.flush()

        if phases < 5:
            return

        p3 = contextlib.ExitStack()
        with p3:
            em.barrier()
            ofm = sb("ofm", [128, NT * 1024], BF16, p3)
            wosb = sb("wosb", [128, NT * DIM], BF16, p3)
            att_b = [sb(f"att{i}", [128, 1024], F32, p3) for i in range(2)]
            wpc_b = [sb(f"wpc{i}", [128, 2048], F32, p3) for i in range(2)]
            tA = sb("tA", [128, 2048], BF16, p3)
            t2o_b = [sb(f"t2o{i}", [128, 2048], BF16, p3) for i in range(2)]

            em.wait("sync", "cq")
            for c in range(NT):
                em.dmat("sync", ofm[:, c * 1024:(c + 1) * 1024],
                        d["o_sc"].ap()[:, c * 128:(c + 1) * 128], "sq")
                em.dma("gp", wosb[:, c * DIM:(c + 1) * DIM],
                       d["woT"][c * 128:(c + 1) * 128, :], "gq")
            em.wait("pe", "sq")
            em.wait("pe", "gq")
            at_done = [0, 0]
            for ob in range(NT):
                b0 = (ob % 2) * 2
                em.wait("pe", "ve")
                for c in range(NT):
                    for jb in range(2):
                        em.op("pe", lambda e, ob=ob, c=c, jb=jb, b0=b0: e.matmul(
                            pb[b0 + jb][:, :],
                            wosb[:, c * DIM + ob * 128: c * DIM + ob * 128 + 128],
                            ofm[:, c * 1024 + jb * 512: c * 1024 + (jb + 1) * 512],
                            start=(c == 0), stop=(c == NT - 1)))
                em.op("pe", lambda e: e.nop(), "pe")
                em.wait("v", "pe")
                at = att_b[ob % 2]
                em.wait("v", "sq", at_done[ob % 2])
                for jb in range(2):
                    em.op("v", lambda e, jb=jb, b0=b0, at=at: e.tensor_copy(
                        at[:, jb * 512:(jb + 1) * 512], pb[b0 + jb][:, :]),
                        "ve" if jb == 1 else None)
                em.wait("sync", "ve")
                em.dma("sync", d["attn_sc"][ob * 128:(ob + 1) * 128, :], at[:, :], "sq")
                at_done[ob % 2] = em.val("sq")

            wp_done = [0, 0]
            for i in range(128):
                bi = i % 2
                wpc = wpc_b[bi]
                em.wait("sync", "ve", wp_done[bi])
                if i < 64:
                    em.dma("sync", wpc[:, :],
                           AP(d["wfcT"], (i // 4) * 128 * HF + (i % 4) * 2048,
                              [[HF, 128], [1, 2048]]), "sq")
                else:
                    em.dma("sync", wpc[:, :],
                           d["wpjT"][(i - 64) * 128:(i - 63) * 128, :], "sq")
                em.wait("v", "sq")
                em.op("v", lambda e, wpc=wpc, i=i: e.tensor_scalar(
                    junk[:, :], wpc[:, :], 0.0, None, OP.max, OP.add,
                    accum_out=gacc[:, 16 + i:17 + i]))
                em.op("v", lambda e, wpc=wpc: e.tensor_scalar(
                    wpc[:, :], wpc[:, :], 0.0, None, OP.min, OP.add,
                    accum_out=tmp4[:, 0:1]), "ve")
                em.vpad(1)
                em.op("v", lambda e, i=i: e.tensor_tensor(
                    gacc[:, 16 + i:17 + i], gacc[:, 16 + i:17 + i], tmp4[:, 0:1],
                    OP.subtract), "ve")
                wp_done[bi] = em.val("ve")
            em.op("v", lambda e: e.tensor_reduce(gacc[:, 0:1], gacc[:, 16:80], AXX, OP.add))
            em.op("v", lambda e: e.tensor_reduce(gacc[:, 1:2], gacc[:, 80:144], AXX, OP.add), "ve")
            em.wait("pe", "ve")
            em.op("pe", lambda e: e.matmul(pb[7][:, 0:2], onesf[:, :], gacc[:, 0:2],
                                           start=True, stop=True))
            em.op("pe", lambda e: e.nop(), "pe")
            em.wait("v", "pe")
            em.op("v", lambda e: e.tensor_scalar_mul(sl["gam_f"], pb[7][:, 0:1], 1.0 / (HF * DIM)))
            em.op("v", lambda e: e.tensor_scalar_mul(sl["gam_p"], pb[7][:, 1:2], 1.0 / (HF * DIM)))
            em.vpad(1)
            em.op("v", lambda e: e.tensor_scalar(sl["th_f"], sl["gam_f"], EPS_RMS, 0.5,
                                                 OP.add, OP.mult))
            em.op("v", lambda e: e.tensor_scalar(sl["th_p"], sl["gam_p"], EPS_RMS, 0.5,
                                                 OP.add, OP.mult))
            em.vpad(1)
            em.op("v", lambda e: e.tensor_scalar_mul(sl["nth_f"], sl["th_f"], -1.0))
            em.op("v", lambda e: e.tensor_scalar_mul(sl["nth_p"], sl["th_p"], -1.0), "ve")

            t2_done = [0, 0]
            for j in range(64):
                bi = j % 2
                wpc, t2o = wpc_b[bi], t2o_b[bi]
                em.wait("sync", "ve", wp_done[bi])
                em.dma("sync", wpc[:, :], d["wpjT"][j * 128:(j + 1) * 128, :], "sq")
                em.wait("v", "sq")
                em.wait("v", "sq", t2_done[bi])
                em.op("v", lambda e, wpc=wpc: e.tensor_scalar(
                    tA[:, :], wpc[:, :], sl["th_p"], None, OP.is_gt, OP.bypass))
                em.op("v", lambda e, wpc=wpc: e.tensor_scalar(
                    junk[:, :], wpc[:, :], sl["nth_p"], None, OP.is_lt, OP.bypass))
                em.op("v", lambda e, t2o=t2o: e.tensor_tensor(
                    t2o[:, :], tA[:, :], junk[:, :], OP.subtract), "ve")
                wp_done[bi] = em.val("ve")
                em.wait("sync", "ve")
                em.dma("sync", d["t2pj_sc"][j * 128:(j + 1) * 128, :], t2o[:, :], "sq")
                t2_done[bi] = em.val("sq")

            em.flush()

        if phases < 6:
            return

        p4 = contextlib.ExitStack()
        with p4:
            em.barrier()
            x2fm = sb("x2fm", [128, NT * 1024], BF16, p4)
            wfp_b = [sb(f"wfp{i}", [128, 512], F32, p4) for i in range(2)]
            t2f_b = [sb(f"t2f{i}", [128, NT * 512], BF16, p4) for i in range(2)]
            ho_b = [sb(f"ho{i}", [128, 1024], BF16, p4) for i in range(2)]
            for c in range(NT):
                em.dmat("sync", x2fm[:, c * 1024:(c + 1) * 1024],
                        d["x2_sc"].ap()[:, c * 128:(c + 1) * 128], "sq")
            em.wait("pe", "sq")
            wf_done = [0, 0]
            tb_done = [0, 0]
            ho_done = [0, 0]
            for obg in range(NT):
                tbuf = t2f_b[obg % 2]
                em.wait("v", "pe", tb_done[obg % 2])
                for c in range(NT):
                    bi = (obg * NT + c) % 2
                    wfp = wfp_b[bi]
                    em.wait("sync", "ve", wf_done[bi])
                    em.dma("sync", wfp[:, :],
                           AP(d["wfcT"], c * 128 * HF + obg * 512,
                              [[HF, 128], [1, 512]]), "sq")
                    em.wait("v", "sq")
                    em.op("v", lambda e, wfp=wfp: e.tensor_scalar(
                        tA[:, 0:512], wfp[:, :], sl["th_f"], None, OP.is_gt, OP.bypass))
                    em.op("v", lambda e, wfp=wfp: e.tensor_scalar(
                        junk[:, 0:512], wfp[:, :], sl["nth_f"], None, OP.is_lt, OP.bypass))
                    em.op("v", lambda e, tbuf=tbuf, c=c: e.tensor_tensor(
                        tbuf[:, c * 512:(c + 1) * 512], tA[:, 0:512], junk[:, 0:512],
                        OP.subtract), "ve")
                    wf_done[bi] = em.val("ve")
                em.wait("pe", "ve")
                for obs in range(4):
                    ob = obg * 4 + obs
                    b0 = (obs % 2) * 2
                    em.wait("pe", "se")
                    for c in range(NT):
                        for jb in range(2):
                            em.op("pe", lambda e, c=c, jb=jb, b0=b0, tbuf=tbuf, obs=obs: e.matmul(
                                pb[b0 + jb][:, :],
                                tbuf[:, c * 512 + obs * 128: c * 512 + obs * 128 + 128],
                                x2fm[:, c * 1024 + jb * 512: c * 1024 + (jb + 1) * 512],
                                start=(c == 0), stop=(c == NT - 1)))
                    em.op("pe", lambda e: e.nop(), "pe")
                    em.wait("sc", "pe")
                    ho = ho_b[obs % 2]
                    em.wait("sc", "cq", ho_done[obs % 2])
                    for jb in range(2):
                        em.op("sc", lambda e, jb=jb, b0=b0, ho=ho: e.activation(
                            ho[:, jb * 512:(jb + 1) * 512], pb[b0 + jb][:, :], ACT.Gelu,
                            scale=sl["gam_f"]), "se")
                    em.wait("sc", "se")
                    em.dma("sc", d["h_sc"][ob * 128:(ob + 1) * 128, :], ho[:, :], "cq")
                    ho_done[obs % 2] = em.val("cq")
                tb_done[obg % 2] = em.val("pe")

            htm_b = [sb(f"htm{i}", [128, HF], BF16, p4) for i in range(2)]
            uqh = sb("uqh", [128, HF // 2], F32, p4)
            hq_done = [0, 0]
            em.wait("sync", "cq")
            for t in range(NU):
                bi = t % 2
                htm = htm_b[bi]
                em.wait("sync", "sq", hq_done[bi])
                em.dmat("sync", htm[:, 0:4096],
                        d["h_sc"].ap()[0:4096, t * 128:(t + 1) * 128], "sq")
                em.dmat("sync", htm[:, 4096:8192],
                        d["h_sc"].ap()[4096:8192, t * 128:(t + 1) * 128], "sq")
                em.wait("v", "sq")
                em.wait("v", "se")
                for q in range(4):
                    em.op("v", lambda e, htm=htm, q=q: e.tensor_scalar(
                        junk[:, :], htm[:, q * 2048:(q + 1) * 2048], 1.0, None,
                        OP.mult, OP.add, accum_out=tmp4[:, q:q + 1]))
                em.vpad(1)
                em.op("v", lambda e: e.tensor_reduce(sl["sum"], tmp4[:, 0:4], AXX, OP.add))
                for q in range(4):
                    em.op("v", lambda e, htm=htm, q=q: ssq_op(
                        e, junk[:, :], htm[:, q * 2048:(q + 1) * 2048], tmp4[:, q:q + 1]))
                em.vpad(1)
                em.op("v", lambda e: e.tensor_reduce(sl["ssq"], tmp4[:, 0:4], AXX, OP.add))
                for q in range(4):
                    em.op("v", lambda e, htm=htm, q=q: e.tensor_scalar(
                        junk[:, :], htm[:, q * 2048:(q + 1) * 2048], 1.0, None,
                        OP.mult, OP.max, accum_out=tmp4[:, q:q + 1]))
                em.vpad(1)
                em.op("v", lambda e: e.tensor_reduce(sl["mx"], tmp4[:, 0:4], AXX, OP.max))
                for q in range(4):
                    em.op("v", lambda e, htm=htm, q=q: e.tensor_scalar(
                        junk[:, :], htm[:, q * 2048:(q + 1) * 2048], 1.0, None,
                        OP.mult, OP.min, accum_out=tmp4[:, q:q + 1]))
                em.vpad(1)
                em.op("v", lambda e: e.tensor_reduce(sl["mn"], tmp4[:, 0:4], AXX, OP.min))
                em.op("v", lambda e: e.tensor_scalar_mul(sl["m2"], sl["sum"], 1.0 / HF))
                em.op("v", lambda e: e.tensor_scalar_mul(sl["t1"], sl["ssq"], 1.0 / HF))
                em.vpad(1)
                em.op("v", lambda e: e.tensor_scalar(sl["var"], sl["m2"], sl["m2"],
                                                     None, OP.mult, OP.bypass))
                em.vpad(2)
                em.op("v", lambda e: e.tensor_tensor(sl["var"], sl["t1"], sl["var"], OP.subtract), "ve")
                em.wait("sc", "ve")
                em.op("sc", lambda e: e.activation(sl["r2"], sl["var"], ACT.Sqrt,
                                                   bias=cpool[:, 1:2], scale=1.0), "se")
                em.wait("v", "se")
                em.op("v", lambda e: e.reciprocal(sl["r2"], sl["r2"]))
                em.op("v", lambda e: e.tensor_scalar(sl["g"], sl["mx"], sl["m2"],
                                                     None, OP.subtract, OP.bypass))
                em.op("v", lambda e: e.tensor_scalar(sl["t1"], sl["mn"], sl["m2"],
                                                     -1.0, OP.subtract, OP.mult))
                em.vpad(2)
                em.op("v", lambda e: e.tensor_tensor(sl["g"], sl["g"], sl["t1"], OP.max))
                em.vpad(2)
                em.op("v", lambda e: e.tensor_scalar(sl["g"], sl["g"], sl["r2"],
                                                     None, OP.mult, OP.bypass))
                em.vpad(2)
                em.op("v", lambda e: e.reciprocal(sl["s1"], sl["g"]))
                em.op("v", lambda e: e.tensor_scalar_mul(sl["s2"], sl["g"], 1.0 / QB))
                em.vpad(1)
                em.op("v", lambda e: e.tensor_scalar(sl["s1"], sl["s1"], sl["r2"],
                                                     QB, OP.mult, OP.mult))
                em.vpad(2)
                em.op("v", lambda e: e.tensor_scalar(sl["t1"], sl["s1"], sl["m2"],
                                                     -1.0, OP.mult, OP.mult), "ve")
                em.wait("sc", "ve")
                for half in range(2):
                    HS = HF // 2
                    em.wait("sc", "ve")
                    em.op("sc", lambda e, htm=htm, half=half, HS=HS: e.activation(
                        uqh[:, :], htm[:, half * HS:(half + 1) * HS], ACT.Identity,
                        bias=sl["t1"], scale=sl["s1"]), "se")
                    em.wait("v", "se")
                    em.op("v", lambda e: e.tensor_scalar(
                        uqh[:, :], uqh[:, :], CLIP_HI, -CLIP_HI, OP.min, OP.max))
                    em.op("v", lambda e, htm=htm, half=half, HS=HS: e.tensor_scalar(
                        htm[:, half * HS:(half + 1) * HS], uqh[:, :], sl["s2"], None,
                        OP.mult, OP.bypass), "ve")
                em.wait("sync", "ve")
                em.dma("sync", d["hq_sc"][t * 128:(t + 1) * 128, :], htm[:, :], "sq")
                hq_done[bi] = em.val("sq")

            em.flush()

        p5 = contextlib.ExitStack()
        with p5:
            em.barrier()
            hqfm = sb("hqfm", [128, 64 * 1024], BF16, p5)
            t2p_b = [sb(f"t2p{i}", [128, 64 * 128], BF16, p5) for i in range(2)]
            ain_b = [sb(f"ain{i}", [128, 1024], F32, p5) for i in range(2)]
            xin_b = [sb(f"xin{i}", [128, 1024], F32, p5) for i in range(2)]
            oo_b = [sb(f"oo{i}", [128, 1024], F32, p5) for i in range(2)]
            for j in range(64):
                em.dmat("sync", hqfm[:, j * 1024:(j + 1) * 1024],
                        d["hq_sc"].ap()[:, j * 128:(j + 1) * 128], "sq")
            em.wait("pe", "sq")
            t2p_used = [0, 0]
            oo_done = [0, 0]
            for ob in range(NT):
                bi = ob % 2
                t2p = t2p_b[bi]
                em.wait("sc", "pe", t2p_used[bi])
                in_ap = AP(d["t2pj_sc"], ob * 128, [[DIM, 128], [128 * DIM, 64], [1, 128]])
                out_ap = AP(t2p, 0, [[64 * 128, 128], [128, 64], [1, 128]])
                em.dma("sc", out_ap, in_ap, "cq")
                em.dma("sc", ain_b[bi][:, :], d["attn_sc"][ob * 128:(ob + 1) * 128, :], "cq")
                em.dma("sc", xin_b[bi][:, :], d["xT"][ob * 128:(ob + 1) * 128, :], "cq")
                em.wait("pe", "cq")
                em.wait("pe", "ve")
                b0 = bi * 2
                for j in range(64):
                    for jb in range(2):
                        em.op("pe", lambda e, j=j, jb=jb, b0=b0, t2p=t2p: e.matmul(
                            pb[b0 + jb][:, :], t2p[:, j * 128:(j + 1) * 128],
                            hqfm[:, j * 1024 + jb * 512: j * 1024 + (jb + 1) * 512],
                            start=(j == 0), stop=(j == 63)))
                em.op("pe", lambda e: e.nop(), "pe")
                t2p_used[bi] = em.val("pe")
                em.wait("v", "pe")
                oo = oo_b[bi]
                em.wait("v", "sq", oo_done[bi])
                for jb in range(2):
                    em.op("v", lambda e, jb=jb, b0=b0, oo=oo, bi=bi: e.scalar_tensor_tensor(
                        oo[:, jb * 512:(jb + 1) * 512], pb[b0 + jb][:, :], sl["gam_p"],
                        ain_b[bi][:, jb * 512:(jb + 1) * 512], OP.mult, OP.add))
                em.op("v", lambda e, oo=oo, bi=bi: e.tensor_tensor(
                    oo[:, :], oo[:, :], xin_b[bi][:, :], OP.add), "ve")
                em.wait("sync", "ve")
                em.dma("sync", d["out"][ob * 128:(ob + 1) * 128, :], oo[:, :], "sq")
                oo_done[bi] = em.val("sq")
            em.barrier()
            em.flush()




def rope_tables():
    inv_freq = (1.0 / (np.float32(10000.0) **
                       (np.arange(0, DR, 2, dtype=np.float32) / np.float32(DR)))).astype(np.float32)
    t = np.arange(S, dtype=np.float32)
    freqs = np.outer(t, inv_freq).astype(np.float32)
    return np.cos(freqs).astype(np.float32), np.sin(freqs).astype(np.float32)


def tile_perm(h):
    return np.concatenate([
        np.concatenate([np.arange(t * 128 + 64 * (1 - h), t * 128 + 64 * (1 - h) + 64),
                        np.arange(t * 128 + 64 * h, t * 128 + 64 * h + 64)])
        for t in range(NT)
    ])


def own_perm(h):
    return tile_perm(h).reshape(NT, 128)[:, 64:].reshape(-1)


def build_masks(h):
    perm = tile_perm(h)
    gk = perm
    gq = own_perm(h)
    mq = np.zeros((128, 256), np.float32)
    for rr in range(128):
        for w in range(256):
            if gk[w] > gq[rr]:
                mq[rr, w] = NEG
    me = np.zeros((128, 128), np.float32)
    mo = np.zeros((128, 128), np.float32)
    for kr in range(128):
        for rr in range(128):
            if gk[kr] > gq[rr]:
                me[kr, rr] = NEG
            if gk[128 + kr] > gq[rr]:
                mo[kr, rr] = NEG
    return mq, me, mo


def stage_inputs(L0_attn_x, L0_attn_wq_down, L0_attn_wq_up, L0_attn_wq_rope,
                 L0_attn_q_norm_w, L0_attn_wkv_down, L0_attn_kv_norm_w,
                 L0_attn_wkv_up, L0_attn_wk_rope, L0_attn_wo,
                 L0_ffn_w_fc, L0_ffn_w_proj):
    cos, sin = rope_tables()
    C = np.ascontiguousarray
    wqd = C(np.asarray(L0_attn_wq_down).T)
    wqu3 = C(np.asarray(L0_attn_wq_up).T.reshape(QR, NH, DN).transpose(1, 0, 2))
    wqr = C(np.asarray(L0_attn_wq_rope).T)
    wkd = C(np.asarray(L0_attn_wkv_down).T)
    wkuT = np.asarray(L0_attn_wkv_up).T.reshape(KVR, NH, DN + DV)
    wkuk = C(wkuT[:, :, :DN].transpose(1, 0, 2))
    wkuv4 = C(wkuT[:, :, DN:].transpose(1, 0, 2).reshape(4, 4, KVR, DV)
              .transpose(0, 2, 1, 3).reshape(4, KVR, 4 * DV))
    wkr = C(np.asarray(L0_attn_wk_rope).T)
    woT = C(np.asarray(L0_attn_wo).T)
    wfcT = C(np.asarray(L0_ffn_w_fc).T)
    wpjT = C(np.asarray(L0_ffn_w_proj).T)
    qnw = C(np.broadcast_to(np.asarray(L0_attn_q_norm_w)[None, :].astype(np.float32),
                            (128, QR)))
    kvnw = C(np.broadcast_to(np.asarray(L0_attn_kv_norm_w)[None, :].astype(np.float32),
                             (128, KVR)))
    ident = np.eye(128, dtype=np.float32)
    onesm = np.ones((128, 128), dtype=np.float32)
    cpool = np.zeros((128, 8), dtype=np.float32)
    cpool[:, 0] = EPS_LN1
    cpool[:, 1] = EPS_LN2
    cpool[:, 2] = EPS_RMS

    ins = []
    for c in range(8):
        b, h = c // 2, c % 2
        xb = np.asarray(L0_attn_x[b], dtype=np.float32)
        perm = tile_perm(h)
        ownp = own_perm(h)
        mq, me, mo = build_masks(h)
        ins.append(dict(
            x=C(xb[perm]), xT=C(xb.T[:, ownp]),
            wqd=wqd, wqu3=wqu3, wqr=wqr, wkd=wkd, wkuk=wkuk, wkuv4=wkuv4,
            wkr=wkr, woT=woT, wfcT=wfcT, wpjT=wpjT, qnw=qnw, kvnw=kvnw,
            cosk=C(cos[perm]), sink=C(sin[perm]),
            cosq=C(cos[ownp]), sinq=C(sin[ownp]),
            ident=ident, onesm=onesm, maskq=mq, maske=me, masko=mo, cpool=cpool,
        ))
    return ins


def kernel(**inputs):
    ins = stage_inputs(**inputs)
    nc = build_nc(taps=DEBUG_TAPS, phases=PHASES)
    res = run_bass_kernel_spmd(nc, ins, core_ids=list(range(8)))
    out = np.empty((4, S, DIM), np.float32)
    for c in range(8):
        b, h = c // 2, c % 2
        o = np.asarray(res.results[c]["out"])
        out[b][own_perm(h)] = o.T
    return out


# version 03 (final)
# speedup vs baseline: 1.1578x; 1.1578x over previous
import math
import contextlib
import numpy as np
import ml_dtypes
import concourse.bass as bass
import concourse.mybir as mybir
from concourse.bass_utils import run_bass_kernel_spmd

F32 = mybir.dt.float32
BF16 = mybir.dt.bfloat16
AXX = mybir.AxisListType.X
OP = mybir.AluOpType
ACT = mybir.ActivationFunctionType
AP = bass.AP

DIM = 2048
S = 2048
NT = 16
NU = 8
NH = 16
QR = 1536
KVR = 512
DN = 128
DR = 64
DV = 128
HF = 8192
EPS_RMS = 1e-6
EPS_LN1 = 1e-5
EPS_LN2 = 1e-6
QB = 128.0
CLIP_HI = 128.0 - 1e-6
NEG = -1.0e30
ISCALE = 1.0 / math.sqrt(DN + DR)

DEBUG_TAPS = ()
PHASES = 99


class Em:
    def __init__(self, nc):
        self.nc = nc
        self.ops = []
        self.sems = {}
        self.es = contextlib.ExitStack()

    def sem(self, name):
        h = self.es.enter_context(self.nc.semaphore(name))
        self.sems[name] = [h, 0]

    def val(self, name):
        return self.sems[name][1]

    def wait(self, eng, sem, val=None):
        v = self.sems[sem][1] if val is None else val
        h = self.sems[sem][0]
        if v > 0:
            self.ops.append((eng, lambda e: e.wait_ge(h, v)))

    def dma(self, q, out, in_, sem, **kw):
        h = self.sems[sem][0]
        self.sems[sem][1] += 16
        self.ops.append((q, lambda e: e.dma_start(out=out, in_=in_, **kw).then_inc(h, 16)))

    def dmat(self, q, out, in_, sem):
        h = self.sems[sem][0]
        self.sems[sem][1] += 16
        self.ops.append((q, lambda e: e.dma_start_transpose(out, in_).then_inc(h, 16)))

    def op(self, eng, fn, sem=None):
        if sem is None:
            self.ops.append((eng, fn))
        else:
            h = self.sems[sem][0]
            self.sems[sem][1] += 1
            self.ops.append((eng, fn))
            self.ops.append((eng, lambda e: e.drain().then_inc(h, 1)))

    def vpad(self, n=2):
        for _ in range(n):
            self.ops.append(("v", lambda e: e.tensor_copy(
                self.padbuf[:, 0:64], self.padbuf[:, 64:128])))

    def barrier(self):
        engs = ("sync", "gp", "sc", "v", "pe")
        sems = ("sq", "gq", "cq", "pe", "ve", "se")
        for e in engs:
            for s in sems:
                self.wait(e, s)

    def run(self, eng_obj, name):
        for e, fn in self.ops:
            if e == name:
                fn(eng_obj)

    def flush(self):
        if not self.ops:
            return
        with self.nc.Block() as block:
            @block.sync
            def _(sync):
                self.run(sync, "sync")

            @block.gpsimd
            def _(gp):
                self.run(gp, "gp")

            @block.scalar
            def _(sc):
                self.run(sc, "sc")

            @block.vector
            def _(v):
                self.run(v, "v")

            @block.tensor
            def _(pe):
                self.run(pe, "pe")
        self.ops = []


def build_nc(taps=(), phases=99):
    nc = bass.Bass("TRN2", target_bir_lowering=False, detect_race_conditions=False)
    d = {}

    def din(name, shape, dtype=F32):
        d[name] = nc.dram_tensor(name, list(shape), dtype, kind="ExternalInput")

    def dsc(name, shape, dtype=BF16):
        kind = "ExternalOutput" if name in taps else "Internal"
        d[name] = nc.dram_tensor(name, list(shape), dtype, kind=kind)

    din("x", [S, DIM]); din("xT", [DIM, 1024])
    din("wqd", [DIM, QR]); din("wqu3", [NH, QR, DN]); din("wqr", [QR, NH * DR])
    din("wkd", [DIM, KVR]); din("wkuk", [NH, KVR, DN]); din("wkuv4", [4, KVR, 4 * DV])
    din("wkr", [DIM, DR]); din("woT", [DIM, DIM])
    din("wfcT", [DIM, HF]); din("wpjT", [HF, DIM])
    din("qnw", [128, QR]); din("kvnw", [128, KVR])
    din("cosk", [S, 32]); din("sink", [S, 32])
    din("cosq", [1024, 32]); din("sinq", [1024, 32])
    din("ident", [128, 128]); din("onesm", [128, 128])
    din("maskq", [128, 256]); din("maske", [128, 128]); din("masko", [128, 128])
    din("cpool", [128, 8])
    d["out"] = nc.dram_tensor("out", [DIM, 1024], F32, kind="ExternalOutput")

    dsc("xn_sc", [S, DIM]); dsc("x2_sc", [1024, DIM]); dsc("xq_sc", [1024, DIM])
    dsc("qc_sc", [1024, QR]); dsc("kvc_sc", [S, KVR])
    dsc("qpe_sc", [1024, NH * 128]); dsc("kro_sc", [S, 128])
    dsc("mrow_sc", [8, 128]); dsc("o_sc", [1024, NH * DV])
    dsc("attn_sc", [DIM, 1024], F32)
    dsc("h_sc", [HF, 1024]); dsc("hq_sc", [1024, HF])
    dsc("t2pj_sc", [HF, DIM])
    dsc("dbg_sc", [128, 64], F32)

    em = Em(nc)
    with em.es:
        _build(nc, em, d, phases)
    return nc


def _build(nc, em, d, phases):
    es = em.es

    def sb(name, shape, dtype=BF16, stack=None):
        return (stack or es).enter_context(nc.sbuf_tensor(name, list(shape), dtype))

    for s in ("sq", "gq", "cq", "pe", "ve", "se"):
        em.sem(s)

    identb = sb("identb", [128, 128])
    identf = sb("identf", [128, 128], F32)
    cpool = sb("cpool_sb", [128, 8], F32)
    onesf = sb("onesf", [128, 128], F32)
    maskqb = sb("maskqb", [128, 256], F32)
    maskeb = sb("maskeb", [128, 128], F32)
    maskob = sb("maskob", [128, 128], F32)
    coskb = sb("coskb", [128, NT * 32], F32)
    sinkb = sb("sinkb", [128, NT * 32], F32)
    cosqb = sb("cosqb", [128, NU * 32], F32)
    sinqb = sb("sinqb", [128, NU * 32], F32)
    stats = sb("stats", [128, 64], F32)
    junk = sb("junk", [128, 2048])
    em.padbuf = sb("padbuf", [128, 128], F32)
    gacc = sb("gacc", [128, 144], F32)
    mhb = sb("mhb", [128, 8])
    mtr = sb("mtr", [8, 128])
    pb = [es.enter_context(nc.psum_tensor(f"pb{i}", [128, 512], F32)) for i in range(8)]

    em.op("v", lambda e: e.memset(em.padbuf[:, :], 0.0))
    em.op("v", lambda e: e.memset(stats[:, :], 0.0))
    em.op("v", lambda e: e.memset(gacc[:, :], 0.0))
    em.op("v", lambda e: e.memset(mhb[:, :], 0.0))
    em.op("v", lambda e: e.memset(mtr[:, :], 0.0))
    em.op("v", lambda e: e.memset(junk[:, :], 0.0), "ve")
    em.dma("gp", identb[:, :], d["ident"][:, :], "gq")
    em.dma("sync", identf[:, :], d["ident"][:, :], "sq")
    em.dma("sync", cpool[:, :], d["cpool"][:, :], "sq")
    em.dma("sync", onesf[:, :], d["onesm"][:, :], "sq")
    em.dma("sync", maskqb[:, :], d["maskq"][:, :], "sq")
    em.dma("sync", maskeb[:, :], d["maske"][:, :], "sq")
    em.dma("sync", maskob[:, :], d["masko"][:, :], "sq")
    for t in range(NT):
        em.dma("sync", coskb[:, t * 32:(t + 1) * 32], d["cosk"][t * 128:(t + 1) * 128, :], "sq")
        em.dma("sync", sinkb[:, t * 32:(t + 1) * 32], d["sink"][t * 128:(t + 1) * 128, :], "sq")
    for u in range(NU):
        em.dma("sync", cosqb[:, u * 32:(u + 1) * 32], d["cosq"][u * 128:(u + 1) * 128, :], "sq")
        em.dma("sync", sinqb[:, u * 32:(u + 1) * 32], d["sinq"][u * 128:(u + 1) * 128, :], "sq")

    sl = {}
    for i, n in enumerate(("sum", "ssq", "mean", "var", "rstd", "t1", "m2", "r2",
                           "mx", "mn", "g", "s1", "s2", "nm", "gam_f", "gam_p",
                           "th_f", "th_p", "nth_f", "nth_p")):
        sl[n] = stats[:, i:i + 1]
    tmp4 = stats[:, 20:24]
    lhat = stats[:, 24:32]
    linv = stats[:, 32:40]
    mhf = stats[:, 40:48]

    def ssq_op(e, out_ap, src, acc):
        return e.scalar_tensor_tensor(out_ap, src, 1.0, src, OP.mult, OP.mult,
                                      accum_out=acc)

    if phases >= 1:
        qnwb = sb("qnwb", [128, QR], F32)
        kvnwb = sb("kvnwb", [128, KVR], F32)
        em.dma("sync", qnwb[:, :], d["qnw"][:, :], "sq")
        em.dma("sync", kvnwb[:, :], d["kvnw"][:, :], "sq")
        p0 = contextlib.ExitStack()
        with p0:
            xt_b = [sb(f"xt{i}", [128, DIM], F32, p0) for i in range(2)]
            xnf_b = [sb(f"xnf{i}", [128, DIM], F32, p0) for i in range(2)]
            uq = sb("uq", [128, DIM], F32, p0)
            x2o_b = [sb(f"x2o{i}", [64, DIM], BF16, p0) for i in range(2)]
            g_done = [0, 0]
            s_done = [0, 0]
            R = slice(64, 128)
            for t in range(NT):
                bi = t % 2
                xt, xnf, x2o = xt_b[bi], xnf_b[bi], x2o_b[bi]
                em.wait("sync", "gq", g_done[bi])
                em.wait("sync", "ve")
                em.dma("sync", xt[:, :], d["x"][t * 128:(t + 1) * 128, :], "sq")
                em.wait("v", "sq")
                em.wait("v", "se")
                em.op("v", lambda e, xt=xt: e.tensor_scalar(
                    junk[:, :], xt[:, :], 1.0, None, OP.mult, OP.add, accum_out=sl["sum"]))
                em.op("v", lambda e, xt=xt: ssq_op(e, junk[:, :], xt[:, :], sl["ssq"]))
                em.vpad(3)
                em.op("v", lambda e: e.tensor_scalar_mul(sl["mean"], sl["sum"], 1.0 / DIM))
                em.op("v", lambda e: e.tensor_scalar_mul(sl["t1"], sl["ssq"], 1.0 / DIM))
                em.vpad(3)
                em.op("v", lambda e: e.tensor_scalar(sl["var"], sl["mean"], sl["mean"],
                                                     None, OP.mult, OP.bypass))
                em.vpad(4)
                em.op("v", lambda e: e.tensor_tensor(sl["var"], sl["t1"], sl["var"], OP.subtract), "ve")
                em.wait("sc", "ve")
                em.op("sc", lambda e: e.activation(sl["rstd"], sl["var"], ACT.Sqrt,
                                                   bias=cpool[:, 0:1], scale=1.0), "se")
                em.wait("v", "se")
                em.op("v", lambda e: e.reciprocal(sl["rstd"], sl["rstd"]))
                em.vpad(4)
                em.op("v", lambda e: e.tensor_scalar(sl["t1"], sl["rstd"], sl["mean"],
                                                     -1.0, OP.mult, OP.mult), "ve")
                em.wait("sc", "ve")
                em.wait("sc", "gq", g_done[bi])
                em.op("sc", lambda e, xt=xt, xnf=xnf: e.activation(
                    xnf[:, :], xt[:, :], ACT.Identity, bias=sl["t1"], scale=sl["rstd"]), "se")
                em.wait("gp", "se")
                em.dma("gp", d["xn_sc"][t * 128:(t + 1) * 128, :], xnf[:, :], "gq")
                em.dma("gp", d["xq_sc"][t * 64:(t + 1) * 64, :], xnf[64:128, :], "gq")
                g_done[bi] = em.val("gq")

                o = xnf[64:128, :]
                em.wait("v", "se")
                em.op("v", lambda e, o=o: e.tensor_scalar(
                    junk[64:128, :], o, 1.0, None, OP.mult, OP.add, accum_out=sl["sum"][R]))
                em.op("v", lambda e, o=o: ssq_op(e, junk[64:128, :], o, sl["ssq"][R]))
                em.op("v", lambda e, o=o: e.tensor_scalar(
                    junk[64:128, :], o, 1.0, None, OP.mult, OP.max, accum_out=sl["mx"][R]))
                em.op("v", lambda e, o=o: e.tensor_scalar(
                    junk[64:128, :], o, 1.0, None, OP.mult, OP.min, accum_out=sl["mn"][R]))
                em.op("v", lambda e: e.tensor_scalar_mul(sl["m2"][R], sl["sum"][R], 1.0 / DIM))
                em.op("v", lambda e: e.tensor_scalar_mul(sl["t1"][R], sl["ssq"][R], 1.0 / DIM))
                em.vpad(3)
                em.op("v", lambda e: e.tensor_scalar(sl["var"][R], sl["m2"][R], sl["m2"][R],
                                                     None, OP.mult, OP.bypass))
                em.vpad(4)
                em.op("v", lambda e: e.tensor_tensor(sl["var"][R], sl["t1"][R], sl["var"][R], OP.subtract), "ve")
                em.wait("sc", "ve")
                em.op("sc", lambda e: e.activation(sl["r2"][R], sl["var"][R], ACT.Sqrt,
                                                   bias=cpool[64:128, 1:2], scale=1.0), "se")
                em.wait("v", "se")
                em.op("v", lambda e: e.reciprocal(sl["r2"][R], sl["r2"][R]))
                em.op("v", lambda e: e.tensor_scalar(sl["g"][R], sl["mx"][R], sl["m2"][R],
                                                     None, OP.subtract, OP.bypass))
                em.op("v", lambda e: e.tensor_scalar(sl["t1"][R], sl["mn"][R], sl["m2"][R],
                                                     -1.0, OP.subtract, OP.mult))
                em.vpad(4)
                em.op("v", lambda e: e.tensor_tensor(sl["g"][R], sl["g"][R], sl["t1"][R], OP.max))
                em.vpad(4)
                em.op("v", lambda e: e.tensor_scalar(sl["g"][R], sl["g"][R], sl["r2"][R],
                                                     None, OP.mult, OP.bypass))
                em.vpad(4)
                em.op("v", lambda e: e.reciprocal(sl["s1"][R], sl["g"][R]))
                em.op("v", lambda e: e.tensor_scalar_mul(sl["s2"][R], sl["g"][R], 1.0 / QB))
                em.vpad(3)
                em.op("v", lambda e: e.tensor_scalar(sl["s1"][R], sl["s1"][R], sl["r2"][R],
                                                     QB, OP.mult, OP.mult))
                em.vpad(4)
                em.op("v", lambda e: e.tensor_scalar(sl["t1"][R], sl["s1"][R], sl["m2"][R],
                                                     -1.0, OP.mult, OP.mult), "ve")
                em.wait("sc", "ve")
                em.op("sc", lambda e, o=o: e.activation(uq[64:128, :], o, ACT.Identity,
                                                        bias=sl["t1"][R], scale=sl["s1"][R]), "se")
                em.wait("v", "se")
                em.wait("v", "sq", s_done[bi])
                em.op("v", lambda e: e.tensor_scalar(uq[64:128, :], uq[64:128, :],
                                                     CLIP_HI, -CLIP_HI, OP.min, OP.max))
                em.op("v", lambda e, x2o=x2o: e.tensor_scalar(
                    x2o[:, :], uq[64:128, :], sl["s2"][R], None, OP.mult, OP.bypass), "ve")
                em.wait("sync", "ve")
                em.dma("sync", d["x2_sc"][t * 64:(t + 1) * 64, :], x2o[:, :], "sq")
                s_done[bi] = em.val("sq")

            em.barrier()
            em.flush()
            if phases < 2:
                return

        p1 = contextlib.ExitStack()
        with p1:
            em.barrier()
            xn_fm = sb("xn_fm", [128, NT * DIM], BF16, p1)
            xq_fm = sb("xq_fm", [128, NT * 1024], BF16, p1)
            wqdb = sb("wqdb", [128, NT * QR], BF16, p1)
            wkdb = sb("wkdb", [128, NT * KVR], BF16, p1)
            wkrb = sb("wkrb", [128, NT * DR], BF16, p1)
            qco_b = [sb(f"qco{i}", [128, QR], BF16, p1) for i in range(2)]
            kvo_b = [sb(f"kvo{i}", [128, KVR], BF16, p1) for i in range(2)]
            kro_b = [sb(f"kro{i}", [128, 128], BF16, p1) for i in range(2)]

            em.wait("sync", "gq")
            for c in range(NT):
                em.dmat("sync", xn_fm[:, c * DIM:(c + 1) * DIM],
                        d["xn_sc"].ap()[:, c * 128:(c + 1) * 128], "sq")
                em.dmat("sync", xq_fm[:, c * 1024:(c + 1) * 1024],
                        d["xq_sc"].ap()[:, c * 128:(c + 1) * 128], "sq")
                em.dma("gp", wqdb[:, c * QR:(c + 1) * QR],
                       d["wqd"][c * 128:(c + 1) * 128, :], "gq")
                em.dma("gp", wkdb[:, c * KVR:(c + 1) * KVR],
                       d["wkd"][c * 128:(c + 1) * 128, :], "gq")
                em.dma("gp", wkrb[:, c * DR:(c + 1) * DR],
                       d["wkr"][c * 128:(c + 1) * 128, :], "gq")
            em.op("v", lambda e: e.memset(kro_b[0][:, 64:128], 0.0))
            em.op("v", lambda e: e.memset(kro_b[1][:, 64:128], 0.0), "ve")
            em.wait("pe", "sq")
            em.wait("pe", "gq")

            qc_done = [0, 0]
            for u in range(NU):
                bi = u % 2
                qco = qco_b[bi]
                for c in range(NT):
                    st, sp = (c == 0), (c == NT - 1)
                    lhs = AP(xq_fm, c * 1024 + u * 128, [[NT * 1024, 128], [1, 128]])
                    for j in range(3):
                        em.op("pe", lambda e, lhs=lhs, c=c, j=j, st=st, sp=sp: e.matmul(
                            pb[j][:, :], lhs,
                            wqdb[:, c * QR + j * 512: c * QR + (j + 1) * 512],
                            start=st, stop=sp))
                em.op("pe", lambda e: e.nop(), "pe")
                em.wait("sc", "pe")
                for j in range(3):
                    em.op("sc", lambda e, j=j: e.activation(
                        junk[:, j * 512:(j + 1) * 512], pb[j][:, :], ACT.Square,
                        accum_out=tmp4[:, j:j + 1]), "se")
                em.wait("v", "se")
                em.op("v", lambda e: e.tensor_reduce(sl["sum"], tmp4[:, 0:3], AXX, OP.add))
                em.vpad(4)
                em.op("v", lambda e: e.tensor_scalar_mul(sl["var"], sl["sum"], 1.0 / QR), "ve")
                em.wait("sc", "ve")
                em.op("sc", lambda e: e.activation(sl["rstd"], sl["var"], ACT.Sqrt,
                                                   bias=cpool[:, 2:3], scale=1.0), "se")
                em.wait("v", "se")
                em.op("v", lambda e: e.reciprocal(sl["rstd"], sl["rstd"]))
                em.vpad(4)
                em.op("v", lambda e: e.tensor_scalar_mul(sl["rstd"], sl["rstd"], ISCALE))
                em.vpad(4)
                em.wait("v", "sq", qc_done[bi])
                for j in range(3):
                    em.op("v", lambda e, j=j, qco=qco: e.scalar_tensor_tensor(
                        qco[:, j * 512:(j + 1) * 512], pb[j][:, :], sl["rstd"],
                        qnwb[:, j * 512:(j + 1) * 512], OP.mult, OP.mult),
                        "ve" if j == 2 else None)
                em.wait("sync", "ve")
                em.dma("sync", d["qc_sc"][u * 128:(u + 1) * 128, :], qco[:, :], "sq")
                qc_done[bi] = em.val("sq")
                em.wait("pe", "ve")

            kv_done = [0, 0]
            kr_done = [0, 0]
            for t in range(NT):
                bi = t % 2
                kvo, kro = kvo_b[bi], kro_b[bi]
                for c in range(NT):
                    st, sp = (c == 0), (c == NT - 1)
                    lhs = AP(xn_fm, c * DIM + t * 128, [[NT * DIM, 128], [1, 128]])
                    em.op("pe", lambda e, lhs=lhs, c=c, st=st, sp=sp: e.matmul(
                        pb[4][:, :], lhs, wkdb[:, c * KVR:(c + 1) * KVR],
                        start=st, stop=sp))
                    em.op("pe", lambda e, lhs=lhs, c=c, st=st, sp=sp: e.matmul(
                        pb[5][:, 0:DR], lhs, wkrb[:, c * DR:(c + 1) * DR],
                        start=st, stop=sp))
                em.op("pe", lambda e: e.nop(), "pe")
                em.wait("sc", "pe")
                em.op("sc", lambda e: e.activation(junk[:, 0:KVR], pb[4][:, :],
                                                   ACT.Square, accum_out=sl["sum"]), "se")
                em.wait("v", "se")
                em.op("v", lambda e: e.tensor_scalar_mul(sl["var"], sl["sum"], 1.0 / KVR), "ve")
                em.wait("sc", "ve")
                em.op("sc", lambda e: e.activation(sl["rstd"], sl["var"], ACT.Sqrt,
                                                   bias=cpool[:, 2:3], scale=1.0), "se")
                em.wait("v", "se")
                em.op("v", lambda e: e.reciprocal(sl["rstd"], sl["rstd"]))
                em.vpad(4)
                em.wait("v", "sq", kv_done[bi])
                em.op("v", lambda e, kvo=kvo: e.scalar_tensor_tensor(
                    kvo[:, :], pb[4][:, :], sl["rstd"], kvnwb[:, :], OP.mult, OP.mult))
                cosap = AP(coskb, t * 32, [[NT * 32, 128], [0, 1], [1, 32]])
                sinap = AP(sinkb, t * 32, [[NT * 32, 128], [0, 1], [1, 32]])

                def pap(off):
                    return AP(pb[5], off, [[512, 128], [64, 1], [2, 32]])

                def kp(tns, off):
                    return AP(tns, off, [[tns.shape[1], 128], [64, 1], [2, 32]])

                em.wait("v", "sq", kr_done[bi])
                em.op("v", lambda e, kro=kro, cosap=cosap: e.tensor_tensor(kp(kro, 0), pap(0), cosap, OP.mult))
                em.op("v", lambda e, kro=kro, cosap=cosap: e.tensor_tensor(kp(kro, 1), pap(1), cosap, OP.mult))
                em.op("v", lambda e, sinap=sinap: e.tensor_tensor(kp(junk, 0), pap(1), sinap, OP.mult))
                em.op("v", lambda e, sinap=sinap: e.tensor_tensor(kp(junk, 1), pap(0), sinap, OP.mult))
                em.op("v", lambda e, kro=kro: e.tensor_tensor(
                    kp(kro, 0), kp(kro, 0), kp(junk, 0), OP.subtract))
                em.op("v", lambda e, kro=kro: e.tensor_tensor(
                    kp(kro, 1), kp(kro, 1), kp(junk, 1), OP.add), "ve")
                em.wait("sync", "ve")
                em.dma("sync", d["kvc_sc"][t * 128:(t + 1) * 128, :], kvo[:, :], "sq")
                kv_done[bi] = em.val("sq")
                em.dma("sync", d["kro_sc"][t * 128:(t + 1) * 128, :], kro[:, :], "sq")
                kr_done[bi] = em.val("sq")
                em.wait("pe", "ve")

            em.flush()

        if phases < 3:
            return

        att = contextlib.ExitStack()
        with att:
            em.barrier()
            qc_fm = sb("qc_fm", [128, 12 * 1024], BF16, att)
            p15 = contextlib.ExitStack()
            with p15:
                wqrb = sb("wqrb", [128, 12 * 1024], BF16, p15)
                qpo_b = [sb(f"qpo{i}", [128, NH * 128], BF16, p15) for i in range(2)]
                for c in range(12):
                    em.dmat("sync", qc_fm[:, c * 1024:(c + 1) * 1024],
                            d["qc_sc"].ap()[:, c * 128:(c + 1) * 128], "sq")
                    em.dma("gp", wqrb[:, c * 1024:(c + 1) * 1024],
                           d["wqr"][c * 128:(c + 1) * 128, :], "gq")
                em.op("v", lambda e: e.memset(qpo_b[0][:, :], 0.0))
                em.op("v", lambda e: e.memset(qpo_b[1][:, :], 0.0), "ve")
                em.wait("pe", "sq")
                em.wait("pe", "gq")
                qp_done = [0, 0]
                for u in range(NU):
                    bi = u % 2
                    qpo = qpo_b[bi]
                    for c in range(12):
                        st, sp = (c == 0), (c == 11)
                        lhs = AP(qc_fm, c * 1024 + u * 128, [[12 * 1024, 128], [1, 128]])
                        em.op("pe", lambda e, lhs=lhs, st=st, sp=sp, c=c: e.matmul(
                            pb[6][:, :], lhs, wqrb[:, c * 1024: c * 1024 + 512],
                            start=st, stop=sp))
                        em.op("pe", lambda e, lhs=lhs, st=st, sp=sp, c=c: e.matmul(
                            pb[7][:, :], lhs, wqrb[:, c * 1024 + 512:(c + 1) * 1024],
                            start=st, stop=sp))
                    em.op("pe", lambda e: e.nop(), "pe")
                    em.wait("v", "pe")
                    em.wait("v", "sq", qp_done[bi])
                    cq_ = AP(cosqb, u * 32, [[NU * 32, 128], [0, 8], [1, 32]])
                    sq_ = AP(sinqb, u * 32, [[NU * 32, 128], [0, 8], [1, 32]])
                    for bk in range(2):
                        def pp(off, bk=bk):
                            return AP(pb[6 + bk], off, [[512, 128], [64, 8], [2, 32]])

                        def qp(off, bk=bk, qpo=qpo):
                            return AP(qpo, bk * 1024 + off, [[NH * 128, 128], [128, 8], [2, 32]])

                        def jp(off):
                            return AP(junk, off, [[2048, 128], [128, 8], [2, 32]])

                        em.op("v", lambda e, qp=qp, pp=pp, cq_=cq_: e.tensor_tensor(qp(0), pp(0), cq_, OP.mult))
                        em.op("v", lambda e, qp=qp, pp=pp, cq_=cq_: e.tensor_tensor(qp(1), pp(1), cq_, OP.mult))
                        em.op("v", lambda e, jp=jp, pp=pp, sq_=sq_: e.tensor_tensor(jp(0), pp(1), sq_, OP.mult))
                        em.op("v", lambda e, jp=jp, pp=pp, sq_=sq_: e.tensor_tensor(jp(1), pp(0), sq_, OP.mult))
                        em.op("v", lambda e, qp=qp, jp=jp: e.tensor_tensor(qp(0), qp(0), jp(0), OP.subtract))
                        em.op("v", lambda e, qp=qp, jp=jp: e.tensor_tensor(qp(1), qp(1), jp(1), OP.add), "ve")
                    em.wait("sync", "ve")
                    em.dma("sync", d["qpe_sc"][u * 128:(u + 1) * 128, :], qpo[:, :], "sq")
                    qp_done[bi] = em.val("sq")
                    em.wait("pe", "ve")

                em.flush()

            if phases < 4:
                return

            em.barrier()
            kvc_fm = sb("kvc_fm", [128, 4 * S], BF16, att)
            kroaug = sb("kroaug", [128, S], BF16, att)
            for c in range(4):
                em.dmat("sync", kvc_fm[:, c * S:(c + 1) * S],
                        d["kvc_sc"].ap()[:, c * 128:(c + 1) * 128], "sq")
            em.dmat("sync", kroaug[:, :], d["kro_sc"].ap()[:, :], "sq")
            em.wait("v", "sq")
            em.op("v", lambda e: e.memset(kroaug[64:65, :], -1.0), "ve")
            em.wait("pe", "sq")
            em.wait("pe", "ve")

            wqu_b = [sb(f"wqu{i}", [128, 12 * 128], BF16, att) for i in range(2)]
            wkuk_b = [sb(f"wkuk{i}", [128, 4 * 128], BF16, att) for i in range(2)]
            qpeT_b = [sb(f"qpeT{i}", [128, 1024], BF16, att) for i in range(2)]
            knop_b = [sb(f"knop{i}", [128, S], BF16, att) for i in range(2)]
            qnop_b = [sb(f"qnop{i}", [128, 1024], BF16, att) for i in range(2)]
            wkuv_sb = sb("wkuv_sb", [128, 4 * 512], BF16, att)
            vgrp = sb("vgrp", [128, NT * 512], BF16, att)
            pt = sb("pt", [128, NT * 1024], BF16, att)
            otile_b = [sb(f"otile{i}", [128, 128], BF16, att) for i in range(2)]
            ot_done = [0, 0]
            pv_done = [0, 0]

            def head_loads(hh):
                bi = hh % 2
                em.wait("gp", "pe")
                in_ap = AP(d["wqu3"], hh * QR * DN, [[DN, 128], [128 * DN, 12], [1, DN]])
                out_ap = AP(wqu_b[bi], 0, [[12 * 128, 128], [128, 12], [1, 128]])
                em.dma("gp", out_ap, in_ap, "gq")
                in_ap = AP(d["wkuk"], hh * KVR * DN, [[DN, 128], [128 * DN, 4], [1, DN]])
                out_ap = AP(wkuk_b[bi], 0, [[4 * 128, 128], [128, 4], [1, 128]])
                em.dma("gp", out_ap, in_ap, "gq")
                em.wait("sc", "pe")
                em.wait("sc", "sq")
                em.dmat("sc", qpeT_b[bi][:, :],
                        d["qpe_sc"].ap()[:, hh * 128:(hh + 1) * 128], "cq")

            def vgroup(g):
                in_ap = AP(d["wkuv4"], g * KVR * 512, [[512, 128], [128 * 512, 4], [1, 512]])
                out_ap = AP(wkuv_sb, 0, [[4 * 512, 128], [512, 4], [1, 512]])
                em.wait("gp", "pe")
                em.dma("gp", out_ap, in_ap, "gq")
                em.wait("pe", "gq")
                em.wait("pe", "se")
                vg_done = [0, 0]
                for tt in range(NT):
                    bank = 6 + tt % 2
                    if tt >= 2:
                        em.wait("pe", "se", vg_done[tt % 2])
                    for c in range(4):
                        lhs = AP(kvc_fm, c * S + tt * 128, [[4 * S, 128], [1, 128]])
                        em.op("pe", lambda e, lhs=lhs, c=c, bank=bank: e.matmul(
                            pb[bank][:, :], lhs, wkuv_sb[:, c * 512:(c + 1) * 512],
                            start=(c == 0), stop=(c == 3)))
                    em.op("pe", lambda e: e.nop(), "pe")
                    em.wait("sc", "pe")
                    em.op("sc", lambda e, tt=tt, bank=bank: e.copy(
                        vgrp[:, tt * 512:(tt + 1) * 512], pb[bank][:, :]), "se")
                    vg_done[tt % 2] = em.val("se")
                em.wait("pe", "se")

            head_loads(0)
            for hh in range(NH):
                bi = hh % 2
                wqu, wkuk_s = wqu_b[bi], wkuk_b[bi]
                qpeT, knop, qnop = qpeT_b[bi], knop_b[bi], qnop_b[bi]
                hin = hh % 4
                if hh % 4 == 0:
                    vgroup(hh // 4)
                if hh + 1 < NH:
                    head_loads(hh + 1)
                em.wait("pe", "gq")
                em.wait("pe", "cq")
                em.wait("pe", "se")
                for c in range(4):
                    for kb in range(4):
                        em.op("pe", lambda e, c=c, kb=kb, wkuk_s=wkuk_s: e.matmul(
                            pb[kb][:, :], wkuk_s[:, c * 128:(c + 1) * 128],
                            kvc_fm[:, c * S + kb * 512: c * S + (kb + 1) * 512],
                            start=(c == 0), stop=(c == 3)))
                em.op("pe", lambda e: e.nop(), "pe")
                em.wait("sc", "pe")
                for kb in range(4):
                    em.op("sc", lambda e, kb=kb, knop=knop: e.copy(
                        knop[:, kb * 512:(kb + 1) * 512], pb[kb][:, :]), "se")
                for c in range(12):
                    for jb in range(2):
                        em.op("pe", lambda e, c=c, jb=jb, wqu=wqu: e.matmul(
                            pb[4 + jb][:, :], wqu[:, c * 128:(c + 1) * 128],
                            qc_fm[:, c * 1024 + jb * 512: c * 1024 + (jb + 1) * 512],
                            start=(c == 0), stop=(c == 11)))
                em.op("pe", lambda e: e.nop(), "pe")
                em.wait("sc", "pe")
                for jb in range(2):
                    em.op("sc", lambda e, jb=jb, qnop=qnop: e.copy(
                        qnop[:, jb * 512:(jb + 1) * 512], pb[4 + jb][:, :]), "se")
                em.wait("pe", "se")

                for u in range(NU):
                    kmax = 256 * (u + 1)
                    nkb = (kmax + 511) // 512
                    for kb in range(nkb):
                        n = min(512, kmax - kb * 512)
                        em.op("pe", lambda e, u=u, kb=kb, n=n, qnop=qnop, knop=knop: e.matmul(
                            pb[kb][:, 0:n], qnop[:, u * 128:(u + 1) * 128],
                            knop[:, kb * 512: kb * 512 + n], start=True, stop=False))
                        em.op("pe", lambda e, u=u, kb=kb, n=n, qpeT=qpeT: e.matmul(
                            pb[kb][:, 0:n], qpeT[0:64, u * 128:(u + 1) * 128],
                            kroaug[0:64, kb * 512: kb * 512 + n], start=False, stop=True))
                    em.op("pe", lambda e: e.nop(), "pe")
                    em.wait("v", "pe")
                    em.wait("v", "se")
                    mb1 = (kmax - 256) // 512
                    off1 = (kmax - 256) % 512
                    if off1 + 256 <= 512:
                        em.op("v", lambda e, mb1=mb1, off1=off1: e.tensor_tensor(
                            pb[mb1][:, off1:off1 + 256], pb[mb1][:, off1:off1 + 256],
                            maskqb[:, :], OP.add))
                    else:
                        w1 = 512 - off1
                        em.op("v", lambda e, mb1=mb1, off1=off1, w1=w1: e.tensor_tensor(
                            pb[mb1][:, off1:512], pb[mb1][:, off1:512],
                            maskqb[:, 0:w1], OP.add))
                        em.op("v", lambda e, mb1=mb1, w1=w1: e.tensor_tensor(
                            pb[mb1 + 1][:, 0:256 - w1], pb[mb1 + 1][:, 0:256 - w1],
                            maskqb[:, w1:256], OP.add))
                    for kb in range(nkb):
                        n = min(512, kmax - kb * 512)
                        em.op("v", lambda e, kb=kb, n=n: e.tensor_reduce(
                            tmp4[:, kb:kb + 1], pb[kb][:, 0:n], AXX, OP.max))
                    em.vpad(3)
                    em.op("v", lambda e, nkb=nkb: e.tensor_reduce(
                        sl["nm"], tmp4[:, 0:nkb], AXX, OP.max, negate=True))
                    em.vpad(4)
                    em.op("v", lambda e, u=u: e.tensor_copy(mhb[:, u:u + 1], sl["nm"]))
                    em.vpad(4)
                    em.op("v", lambda e, u=u: e.tensor_copy(mhf[:, u:u + 1], mhb[:, u:u + 1]))
                    em.vpad(4)
                    em.op("v", lambda e, u=u: e.tensor_copy(sl["nm"], mhf[:, u:u + 1]), "ve")
                    em.wait("sc", "ve")
                    for kb in range(nkb):
                        n = min(512, kmax - kb * 512)
                        em.op("sc", lambda e, kb=kb, n=n: e.activation(
                            junk[:, kb * 512:kb * 512 + n], pb[kb][:, 0:n], ACT.Exp,
                            bias=sl["nm"], scale=1.0, accum_out=tmp4[:, kb:kb + 1]), "se")
                    em.wait("v", "se")
                    em.op("v", lambda e, u=u, nkb=nkb: e.tensor_reduce(
                        lhat[:, u:u + 1], tmp4[:, 0:nkb], AXX, OP.add), "ve")
                    em.wait("pe", "ve")
                em.vpad(4)
                em.op("v", lambda e: e.reciprocal(linv[:, :], lhat[:, :]))
                em.op("v", lambda e: e.tensor_scalar_mul(mhf[:, :], mhf[:, :], -1.0), "ve")
                em.wait("pe", "ve")
                em.op("pe", lambda e: e.transpose(pb[7][0:8, 0:128], mhf[:, :], identf[:, :]))
                em.op("pe", lambda e: e.nop(), "pe")
                em.wait("v", "pe")
                em.op("v", lambda e: e.tensor_copy(mtr[:, :], pb[7][0:8, 0:128]), "ve")
                em.wait("sc", "ve")
                em.dma("sc", d["mrow_sc"][:, :], mtr[:, :], "cq")
                em.wait("sc", "cq")
                em.dma("sc", qpeT[64:65, :], AP(d["mrow_sc"], 0, [[1024, 1], [1, 1024]]), "cq")
                em.wait("pe", "cq")

                for kt in range(NT):
                    u0 = kt // 2
                    q0 = u0 * 128
                    wq_ = 1024 - q0
                    nqb = (wq_ + 511) // 512
                    for qi in range(nqb):
                        n = min(512, wq_ - qi * 512)
                        bank = 4 + qi % 2
                        em.op("pe", lambda e, kt=kt, q0=q0, qi=qi, n=n, bank=bank,
                              knop=knop, qnop=qnop: e.matmul(
                                  pb[bank][:, 0:n], knop[:, kt * 128:(kt + 1) * 128],
                                  qnop[:, q0 + qi * 512: q0 + qi * 512 + n],
                                  start=True, stop=False))
                        em.op("pe", lambda e, kt=kt, q0=q0, qi=qi, n=n, bank=bank,
                              qpeT=qpeT: e.matmul(
                                  pb[bank][:, 0:n], kroaug[0:65, kt * 128:(kt + 1) * 128],
                                  qpeT[0:65, q0 + qi * 512: q0 + qi * 512 + n],
                                  start=False, stop=True))
                    em.op("pe", lambda e: e.nop(), "pe")
                    em.wait("v", "pe")
                    mk = maskeb if kt % 2 == 0 else maskob
                    em.op("v", lambda e, mk=mk: e.tensor_tensor(
                        pb[4][:, 0:128], pb[4][:, 0:128], mk[:, :], OP.add), "ve")
                    em.wait("sc", "ve")
                    for qi in range(nqb):
                        n = min(512, wq_ - qi * 512)
                        bank = 4 + qi % 2
                        em.op("sc", lambda e, kt=kt, q0=q0, qi=qi, n=n, bank=bank: e.activation(
                            pt[:, kt * 1024 + q0 + qi * 512: kt * 1024 + q0 + qi * 512 + n],
                            pb[bank][:, 0:n], ACT.Exp), "se")
                    em.wait("pe", "se")

                for u in range(NU):
                    bank = 6 + u % 2
                    nkt = 2 * u + 2
                    em.wait("pe", "se", pv_done[u % 2])
                    for kt in range(nkt):
                        em.op("pe", lambda e, u=u, kt=kt, bank=bank, hin=hin,
                              nkt=nkt: e.matmul(
                            pb[bank][:, 0:128],
                            pt[:, kt * 1024 + u * 128: kt * 1024 + u * 128 + 128],
                            vgrp[:, kt * 512 + hin * 128: kt * 512 + hin * 128 + 128],
                            start=(kt == 0), stop=(kt == nkt - 1)))
                    em.op("pe", lambda e: e.nop(), "pe")
                    em.wait("sc", "pe")
                    ot = otile_b[u % 2]
                    em.wait("sc", "cq", ot_done[u % 2])
                    em.op("sc", lambda e, u=u, bank=bank, ot=ot: e.activation(
                        ot[:, :], pb[bank][:, 0:128], ACT.Identity,
                        scale=linv[:, u:u + 1]), "se")
                    pv_done[u % 2] = em.val("se")
                    em.wait("sc", "se")
                    em.dma("sc", d["o_sc"][u * 128:(u + 1) * 128, hh * DV:(hh + 1) * DV],
                           ot[:, :], "cq")
                    ot_done[u % 2] = em.val("cq")
                em.wait("pe", "cq", ot_done[0])

            em.flush()

        if phases < 5:
            return

        p3 = contextlib.ExitStack()
        with p3:
            em.barrier()
            ofm = sb("ofm", [128, NT * 1024], BF16, p3)
            wosb = sb("wosb", [128, NT * DIM], BF16, p3)
            att_b = [sb(f"att{i}", [128, 1024], F32, p3) for i in range(2)]
            wpc_b = [sb(f"wpc{i}", [128, 2048], F32, p3) for i in range(2)]
            tA = sb("tA", [128, 2048], BF16, p3)
            t2o_b = [sb(f"t2o{i}", [128, 2048], BF16, p3) for i in range(2)]

            em.wait("sync", "cq")
            for c in range(NT):
                em.dmat("sync", ofm[:, c * 1024:(c + 1) * 1024],
                        d["o_sc"].ap()[:, c * 128:(c + 1) * 128], "sq")
                em.dma("gp", wosb[:, c * DIM:(c + 1) * DIM],
                       d["woT"][c * 128:(c + 1) * 128, :], "gq")
            em.wait("pe", "sq")
            em.wait("pe", "gq")
            at_done = [0, 0]
            for ob in range(NT):
                b0 = (ob % 2) * 2
                em.wait("pe", "ve")
                for c in range(NT):
                    for jb in range(2):
                        em.op("pe", lambda e, ob=ob, c=c, jb=jb, b0=b0: e.matmul(
                            pb[b0 + jb][:, :],
                            wosb[:, c * DIM + ob * 128: c * DIM + ob * 128 + 128],
                            ofm[:, c * 1024 + jb * 512: c * 1024 + (jb + 1) * 512],
                            start=(c == 0), stop=(c == NT - 1)))
                em.op("pe", lambda e: e.nop(), "pe")
                em.wait("v", "pe")
                at = att_b[ob % 2]
                em.wait("v", "sq", at_done[ob % 2])
                for jb in range(2):
                    em.op("v", lambda e, jb=jb, b0=b0, at=at: e.tensor_copy(
                        at[:, jb * 512:(jb + 1) * 512], pb[b0 + jb][:, :]),
                        "ve" if jb == 1 else None)
                em.wait("sync", "ve")
                em.dma("sync", d["attn_sc"][ob * 128:(ob + 1) * 128, :], at[:, :], "sq")
                at_done[ob % 2] = em.val("sq")

            wp_done = [0, 0]
            for i in range(128):
                bi = i % 2
                wpc = wpc_b[bi]
                em.wait("sync", "ve", wp_done[bi])
                if i < 64:
                    em.dma("sync", wpc[:, :],
                           AP(d["wfcT"], (i // 4) * 128 * HF + (i % 4) * 2048,
                              [[HF, 128], [1, 2048]]), "sq")
                else:
                    em.dma("sync", wpc[:, :],
                           d["wpjT"][(i - 64) * 128:(i - 63) * 128, :], "sq")
                em.wait("v", "sq")
                em.op("v", lambda e, wpc=wpc, i=i: e.tensor_scalar(
                    junk[:, :], wpc[:, :], 0.0, None, OP.max, OP.add,
                    accum_out=gacc[:, 16 + i:17 + i]))
                em.op("v", lambda e, wpc=wpc: e.tensor_scalar(
                    wpc[:, :], wpc[:, :], 0.0, None, OP.min, OP.add,
                    accum_out=tmp4[:, 0:1]), "ve")
                em.vpad(3)
                em.op("v", lambda e, i=i: e.tensor_tensor(
                    gacc[:, 16 + i:17 + i], gacc[:, 16 + i:17 + i], tmp4[:, 0:1],
                    OP.subtract), "ve")
                wp_done[bi] = em.val("ve")
            em.op("v", lambda e: e.tensor_reduce(gacc[:, 0:1], gacc[:, 16:80], AXX, OP.add))
            em.op("v", lambda e: e.tensor_reduce(gacc[:, 1:2], gacc[:, 80:144], AXX, OP.add), "ve")
            em.wait("pe", "ve")
            em.op("pe", lambda e: e.matmul(pb[7][:, 0:2], onesf[:, :], gacc[:, 0:2],
                                           start=True, stop=True))
            em.op("pe", lambda e: e.nop(), "pe")
            em.wait("v", "pe")
            em.op("v", lambda e: e.tensor_scalar_mul(sl["gam_f"], pb[7][:, 0:1], 1.0 / (HF * DIM)))
            em.op("v", lambda e: e.tensor_scalar_mul(sl["gam_p"], pb[7][:, 1:2], 1.0 / (HF * DIM)))
            em.vpad(3)
            em.op("v", lambda e: e.tensor_scalar(sl["th_f"], sl["gam_f"], EPS_RMS, 0.5,
                                                 OP.add, OP.mult))
            em.op("v", lambda e: e.tensor_scalar(sl["th_p"], sl["gam_p"], EPS_RMS, 0.5,
                                                 OP.add, OP.mult))
            em.vpad(3)
            em.op("v", lambda e: e.tensor_scalar_mul(sl["nth_f"], sl["th_f"], -1.0))
            em.op("v", lambda e: e.tensor_scalar_mul(sl["nth_p"], sl["th_p"], -1.0), "ve")

            t2_done = [0, 0]
            for j in range(64):
                bi = j % 2
                wpc, t2o = wpc_b[bi], t2o_b[bi]
                em.wait("sync", "ve", wp_done[bi])
                em.dma("sync", wpc[:, :], d["wpjT"][j * 128:(j + 1) * 128, :], "sq")
                em.wait("v", "sq")
                em.wait("v", "sq", t2_done[bi])
                em.op("v", lambda e, wpc=wpc: e.tensor_scalar(
                    tA[:, :], wpc[:, :], sl["th_p"], None, OP.is_gt, OP.bypass))
                em.op("v", lambda e, wpc=wpc: e.tensor_scalar(
                    junk[:, :], wpc[:, :], sl["nth_p"], None, OP.is_lt, OP.bypass))
                em.op("v", lambda e, t2o=t2o: e.tensor_tensor(
                    t2o[:, :], tA[:, :], junk[:, :], OP.subtract), "ve")
                wp_done[bi] = em.val("ve")
                em.wait("sync", "ve")
                em.dma("sync", d["t2pj_sc"][j * 128:(j + 1) * 128, :], t2o[:, :], "sq")
                t2_done[bi] = em.val("sq")

            em.flush()

        if phases < 6:
            return

        p4 = contextlib.ExitStack()
        with p4:
            em.barrier()
            x2fm = sb("x2fm", [128, NT * 1024], BF16, p4)
            wfp_b = [sb(f"wfp{i}", [128, 512], F32, p4) for i in range(2)]
            t2f_b = [sb(f"t2f{i}", [128, NT * 512], BF16, p4) for i in range(2)]
            ho_b = [sb(f"ho{i}", [128, 1024], BF16, p4) for i in range(2)]
            for c in range(NT):
                em.dmat("sync", x2fm[:, c * 1024:(c + 1) * 1024],
                        d["x2_sc"].ap()[:, c * 128:(c + 1) * 128], "sq")
            em.wait("pe", "sq")
            wf_done = [0, 0]
            tb_done = [0, 0]
            ho_done = [0, 0]
            for obg in range(NT):
                tbuf = t2f_b[obg % 2]
                em.wait("v", "pe", tb_done[obg % 2])
                for c in range(NT):
                    bi = (obg * NT + c) % 2
                    wfp = wfp_b[bi]
                    em.wait("sync", "ve", wf_done[bi])
                    em.dma("sync", wfp[:, :],
                           AP(d["wfcT"], c * 128 * HF + obg * 512,
                              [[HF, 128], [1, 512]]), "sq")
                    em.wait("v", "sq")
                    em.op("v", lambda e, wfp=wfp: e.tensor_scalar(
                        tA[:, 0:512], wfp[:, :], sl["th_f"], None, OP.is_gt, OP.bypass))
                    em.op("v", lambda e, wfp=wfp: e.tensor_scalar(
                        junk[:, 0:512], wfp[:, :], sl["nth_f"], None, OP.is_lt, OP.bypass))
                    em.op("v", lambda e, tbuf=tbuf, c=c: e.tensor_tensor(
                        tbuf[:, c * 512:(c + 1) * 512], tA[:, 0:512], junk[:, 0:512],
                        OP.subtract), "ve")
                    wf_done[bi] = em.val("ve")
                em.wait("pe", "ve")
                for obs in range(4):
                    ob = obg * 4 + obs
                    b0 = (obs % 2) * 2
                    em.wait("pe", "se")
                    for c in range(NT):
                        for jb in range(2):
                            em.op("pe", lambda e, c=c, jb=jb, b0=b0, tbuf=tbuf, obs=obs: e.matmul(
                                pb[b0 + jb][:, :],
                                tbuf[:, c * 512 + obs * 128: c * 512 + obs * 128 + 128],
                                x2fm[:, c * 1024 + jb * 512: c * 1024 + (jb + 1) * 512],
                                start=(c == 0), stop=(c == NT - 1)))
                    em.op("pe", lambda e: e.nop(), "pe")
                    em.wait("sc", "pe")
                    ho = ho_b[obs % 2]
                    em.wait("sc", "cq", ho_done[obs % 2])
                    for jb in range(2):
                        em.op("sc", lambda e, jb=jb, b0=b0, ho=ho: e.activation(
                            ho[:, jb * 512:(jb + 1) * 512], pb[b0 + jb][:, :], ACT.Gelu,
                            scale=sl["gam_f"]), "se")
                    em.wait("sc", "se")
                    em.dma("sc", d["h_sc"][ob * 128:(ob + 1) * 128, :], ho[:, :], "cq")
                    ho_done[obs % 2] = em.val("cq")
                tb_done[obg % 2] = em.val("pe")

            htm_b = [sb(f"htm{i}", [128, HF], BF16, p4) for i in range(2)]
            uqh = sb("uqh", [128, HF // 2], F32, p4)
            hq_done = [0, 0]
            em.wait("sync", "cq")
            for t in range(NU):
                bi = t % 2
                htm = htm_b[bi]
                em.wait("sync", "sq", hq_done[bi])
                em.dmat("sync", htm[:, 0:4096],
                        d["h_sc"].ap()[0:4096, t * 128:(t + 1) * 128], "sq")
                em.dmat("sync", htm[:, 4096:8192],
                        d["h_sc"].ap()[4096:8192, t * 128:(t + 1) * 128], "sq")
                em.wait("v", "sq")
                em.wait("v", "se")
                for q in range(4):
                    em.op("v", lambda e, htm=htm, q=q: e.tensor_scalar(
                        junk[:, :], htm[:, q * 2048:(q + 1) * 2048], 1.0, None,
                        OP.mult, OP.add, accum_out=tmp4[:, q:q + 1]))
                em.vpad(3)
                em.op("v", lambda e: e.tensor_reduce(sl["sum"], tmp4[:, 0:4], AXX, OP.add))
                for q in range(4):
                    em.op("v", lambda e, htm=htm, q=q: ssq_op(
                        e, junk[:, :], htm[:, q * 2048:(q + 1) * 2048], tmp4[:, q:q + 1]))
                em.vpad(3)
                em.op("v", lambda e: e.tensor_reduce(sl["ssq"], tmp4[:, 0:4], AXX, OP.add))
                for q in range(4):
                    em.op("v", lambda e, htm=htm, q=q: e.tensor_scalar(
                        junk[:, :], htm[:, q * 2048:(q + 1) * 2048], 1.0, None,
                        OP.mult, OP.max, accum_out=tmp4[:, q:q + 1]))
                em.vpad(3)
                em.op("v", lambda e: e.tensor_reduce(sl["mx"], tmp4[:, 0:4], AXX, OP.max))
                for q in range(4):
                    em.op("v", lambda e, htm=htm, q=q: e.tensor_scalar(
                        junk[:, :], htm[:, q * 2048:(q + 1) * 2048], 1.0, None,
                        OP.mult, OP.min, accum_out=tmp4[:, q:q + 1]))
                em.vpad(3)
                em.op("v", lambda e: e.tensor_reduce(sl["mn"], tmp4[:, 0:4], AXX, OP.min))
                em.op("v", lambda e: e.tensor_scalar_mul(sl["m2"], sl["sum"], 1.0 / HF))
                em.op("v", lambda e: e.tensor_scalar_mul(sl["t1"], sl["ssq"], 1.0 / HF))
                em.vpad(3)
                em.op("v", lambda e: e.tensor_scalar(sl["var"], sl["m2"], sl["m2"],
                                                     None, OP.mult, OP.bypass))
                em.vpad(4)
                em.op("v", lambda e: e.tensor_tensor(sl["var"], sl["t1"], sl["var"], OP.subtract), "ve")
                em.wait("sc", "ve")
                em.op("sc", lambda e: e.activation(sl["r2"], sl["var"], ACT.Sqrt,
                                                   bias=cpool[:, 1:2], scale=1.0), "se")
                em.wait("v", "se")
                em.op("v", lambda e: e.reciprocal(sl["r2"], sl["r2"]))
                em.op("v", lambda e: e.tensor_scalar(sl["g"], sl["mx"], sl["m2"],
                                                     None, OP.subtract, OP.bypass))
                em.op("v", lambda e: e.tensor_scalar(sl["t1"], sl["mn"], sl["m2"],
                                                     -1.0, OP.subtract, OP.mult))
                em.vpad(4)
                em.op("v", lambda e: e.tensor_tensor(sl["g"], sl["g"], sl["t1"], OP.max))
                em.vpad(4)
                em.op("v", lambda e: e.tensor_scalar(sl["g"], sl["g"], sl["r2"],
                                                     None, OP.mult, OP.bypass))
                em.vpad(4)
                em.op("v", lambda e: e.reciprocal(sl["s1"], sl["g"]))
                em.op("v", lambda e: e.tensor_scalar_mul(sl["s2"], sl["g"], 1.0 / QB))
                em.vpad(3)
                em.op("v", lambda e: e.tensor_scalar(sl["s1"], sl["s1"], sl["r2"],
                                                     QB, OP.mult, OP.mult))
                em.vpad(4)
                em.op("v", lambda e: e.tensor_scalar(sl["t1"], sl["s1"], sl["m2"],
                                                     -1.0, OP.mult, OP.mult), "ve")
                em.wait("sc", "ve")
                for half in range(2):
                    HS = HF // 2
                    em.wait("sc", "ve")
                    em.op("sc", lambda e, htm=htm, half=half, HS=HS: e.activation(
                        uqh[:, :], htm[:, half * HS:(half + 1) * HS], ACT.Identity,
                        bias=sl["t1"], scale=sl["s1"]), "se")
                    em.wait("v", "se")
                    em.op("v", lambda e: e.tensor_scalar(
                        uqh[:, :], uqh[:, :], CLIP_HI, -CLIP_HI, OP.min, OP.max))
                    em.op("v", lambda e, htm=htm, half=half, HS=HS: e.tensor_scalar(
                        htm[:, half * HS:(half + 1) * HS], uqh[:, :], sl["s2"], None,
                        OP.mult, OP.bypass), "ve")
                em.wait("sync", "ve")
                em.dma("sync", d["hq_sc"][t * 128:(t + 1) * 128, :], htm[:, :], "sq")
                hq_done[bi] = em.val("sq")

            em.flush()

        p5 = contextlib.ExitStack()
        with p5:
            em.barrier()
            hqfm = sb("hqfm", [128, 64 * 1024], BF16, p5)
            t2p_b = [sb(f"t2p{i}", [128, 64 * 128], BF16, p5) for i in range(2)]
            ain_b = [sb(f"ain{i}", [128, 1024], F32, p5) for i in range(2)]
            xin_b = [sb(f"xin{i}", [128, 1024], F32, p5) for i in range(2)]
            oo_b = [sb(f"oo{i}", [128, 1024], F32, p5) for i in range(2)]
            for j in range(64):
                em.dmat("sync", hqfm[:, j * 1024:(j + 1) * 1024],
                        d["hq_sc"].ap()[:, j * 128:(j + 1) * 128], "sq")
            em.wait("pe", "sq")
            t2p_used = [0, 0]
            oo_done = [0, 0]
            for ob in range(NT):
                bi = ob % 2
                t2p = t2p_b[bi]
                em.wait("sc", "pe", t2p_used[bi])
                in_ap = AP(d["t2pj_sc"], ob * 128, [[DIM, 128], [128 * DIM, 64], [1, 128]])
                out_ap = AP(t2p, 0, [[64 * 128, 128], [128, 64], [1, 128]])
                em.dma("sc", out_ap, in_ap, "cq")
                em.dma("sc", ain_b[bi][:, :], d["attn_sc"][ob * 128:(ob + 1) * 128, :], "cq")
                em.dma("sc", xin_b[bi][:, :], d["xT"][ob * 128:(ob + 1) * 128, :], "cq")
                em.wait("pe", "cq")
                em.wait("pe", "ve")
                b0 = bi * 2
                for j in range(64):
                    for jb in range(2):
                        em.op("pe", lambda e, j=j, jb=jb, b0=b0, t2p=t2p: e.matmul(
                            pb[b0 + jb][:, :], t2p[:, j * 128:(j + 1) * 128],
                            hqfm[:, j * 1024 + jb * 512: j * 1024 + (jb + 1) * 512],
                            start=(j == 0), stop=(j == 63)))
                em.op("pe", lambda e: e.nop(), "pe")
                t2p_used[bi] = em.val("pe")
                em.wait("v", "pe")
                oo = oo_b[bi]
                em.wait("v", "sq", oo_done[bi])
                for jb in range(2):
                    em.op("v", lambda e, jb=jb, b0=b0, oo=oo, bi=bi: e.scalar_tensor_tensor(
                        oo[:, jb * 512:(jb + 1) * 512], pb[b0 + jb][:, :], sl["gam_p"],
                        ain_b[bi][:, jb * 512:(jb + 1) * 512], OP.mult, OP.add))
                em.op("v", lambda e, oo=oo, bi=bi: e.tensor_tensor(
                    oo[:, :], oo[:, :], xin_b[bi][:, :], OP.add), "ve")
                em.wait("sync", "ve")
                em.dma("sync", d["out"][ob * 128:(ob + 1) * 128, :], oo[:, :], "sq")
                oo_done[bi] = em.val("sq")
            em.barrier()
            em.flush()




def rope_tables():
    inv_freq = (1.0 / (np.float32(10000.0) **
                       (np.arange(0, DR, 2, dtype=np.float32) / np.float32(DR)))).astype(np.float32)
    t = np.arange(S, dtype=np.float32)
    freqs = np.outer(t, inv_freq).astype(np.float32)
    return np.cos(freqs).astype(np.float32), np.sin(freqs).astype(np.float32)


def tile_perm(h):
    return np.concatenate([
        np.concatenate([np.arange(t * 128 + 64 * (1 - h), t * 128 + 64 * (1 - h) + 64),
                        np.arange(t * 128 + 64 * h, t * 128 + 64 * h + 64)])
        for t in range(NT)
    ])


def own_perm(h):
    return tile_perm(h).reshape(NT, 128)[:, 64:].reshape(-1)


def build_masks(h):
    perm = tile_perm(h)
    gk = perm
    gq = own_perm(h)
    mq = np.zeros((128, 256), np.float32)
    for rr in range(128):
        for w in range(256):
            if gk[w] > gq[rr]:
                mq[rr, w] = NEG
    me = np.zeros((128, 128), np.float32)
    mo = np.zeros((128, 128), np.float32)
    for kr in range(128):
        for rr in range(128):
            if gk[kr] > gq[rr]:
                me[kr, rr] = NEG
            if gk[128 + kr] > gq[rr]:
                mo[kr, rr] = NEG
    return mq, me, mo


def stage_inputs(L0_attn_x, L0_attn_wq_down, L0_attn_wq_up, L0_attn_wq_rope,
                 L0_attn_q_norm_w, L0_attn_wkv_down, L0_attn_kv_norm_w,
                 L0_attn_wkv_up, L0_attn_wk_rope, L0_attn_wo,
                 L0_ffn_w_fc, L0_ffn_w_proj):
    cos, sin = rope_tables()
    C = np.ascontiguousarray
    wqd = C(np.asarray(L0_attn_wq_down).T)
    wqu3 = C(np.asarray(L0_attn_wq_up).T.reshape(QR, NH, DN).transpose(1, 0, 2))
    wqr = C(np.asarray(L0_attn_wq_rope).T)
    wkd = C(np.asarray(L0_attn_wkv_down).T)
    wkuT = np.asarray(L0_attn_wkv_up).T.reshape(KVR, NH, DN + DV)
    wkuk = C(wkuT[:, :, :DN].transpose(1, 0, 2))
    wkuv4 = C(wkuT[:, :, DN:].transpose(1, 0, 2).reshape(4, 4, KVR, DV)
              .transpose(0, 2, 1, 3).reshape(4, KVR, 4 * DV))
    wkr = C(np.asarray(L0_attn_wk_rope).T)
    woT = C(np.asarray(L0_attn_wo).T)
    wfcT = C(np.asarray(L0_ffn_w_fc).T)
    wpjT = C(np.asarray(L0_ffn_w_proj).T)
    qnw = C(np.broadcast_to(np.asarray(L0_attn_q_norm_w)[None, :].astype(np.float32),
                            (128, QR)))
    kvnw = C(np.broadcast_to(np.asarray(L0_attn_kv_norm_w)[None, :].astype(np.float32),
                             (128, KVR)))
    ident = np.eye(128, dtype=np.float32)
    onesm = np.ones((128, 128), dtype=np.float32)
    cpool = np.zeros((128, 8), dtype=np.float32)
    cpool[:, 0] = EPS_LN1
    cpool[:, 1] = EPS_LN2
    cpool[:, 2] = EPS_RMS

    ins = []
    for c in range(8):
        b, h = c // 2, c % 2
        xb = np.asarray(L0_attn_x[b], dtype=np.float32)
        perm = tile_perm(h)
        ownp = own_perm(h)
        mq, me, mo = build_masks(h)
        ins.append(dict(
            x=C(xb[perm]), xT=C(xb.T[:, ownp]),
            wqd=wqd, wqu3=wqu3, wqr=wqr, wkd=wkd, wkuk=wkuk, wkuv4=wkuv4,
            wkr=wkr, woT=woT, wfcT=wfcT, wpjT=wpjT, qnw=qnw, kvnw=kvnw,
            cosk=C(cos[perm]), sink=C(sin[perm]),
            cosq=C(cos[ownp]), sinq=C(sin[ownp]),
            ident=ident, onesm=onesm, maskq=mq, maske=me, masko=mo, cpool=cpool,
        ))
    return ins


def kernel(**inputs):
    ins = stage_inputs(**inputs)
    nc = build_nc(taps=DEBUG_TAPS, phases=PHASES)
    res = run_bass_kernel_spmd(nc, ins, core_ids=list(range(8)))
    out = np.empty((4, S, DIM), np.float32)
    for c in range(8):
        b, h = c // 2, c % 2
        o = np.asarray(res.results[c]["out"])
        out[b][own_perm(h)] = o.T
    return out
